# version 16
# baseline (speedup 1.0000x reference)
import contextlib
import numpy as np
import concourse.bass as bass
import concourse.mybir as mybir
from concourse.bass_utils import run_bass_kernel_spmd

F32 = mybir.dt.float32
BF16 = mybir.dt.bfloat16
AF = mybir.ActivationFunctionType
ALU = mybir.AluOpType
AX = mybir.AxisListType

NC = 8
S = 16384
D = 2048
R = S // NC
G = 8192
HF = 8192
NT = R // 128
NKD = D // 128
NG = G // 128
NF = HF // 128
NCH = 4
HEADS = 4

C_EPS_RMS, C_EPS_LN, C_NEG_HALF, C_INV512, C_INV2048 = 0, 1, 2, 3, 4


class Sem:
    __slots__ = ("name", "n", "h", "owner")

    def __init__(self, name, owner=None):
        self.name, self.n, self.h, self.owner = name, 0, None, owner


class Rec:
    ENGS = ("sync", "gpsimd", "vector", "scalar", "tensor")

    def __init__(self):
        self.ops = {e: [] for e in self.ENGS}
        self.sems = []

    def sem(self, name, owner=None):
        s = Sem(name, owner)
        self.sems.append(s)
        return s

    def dsems(self, name, k):
        return [self.sem(f"{name}{i}") for i in range(k)]

    def add(self, eng, fn, waits=(), inc=None, amt=1):
        tok = None
        if inc is not None:
            inc.n += amt
            tok = (inc, inc.n)
        self.ops[eng].append((fn, [w for w in waits if w is not None], inc, amt))
        return tok

    def final_dma_waits(self):
        return [(s, s.n) for s in self.sems if s.owner is None and s.n > 0]

    def emit(self, nc, block):
        ctx = contextlib.ExitStack()
        for s in self.sems:
            s.h = ctx.enter_context(nc.semaphore(s.name))

        def make_body(ename):
            def body(eng):
                waited = {}
                for fn, ws, inc, amt in self.ops[ename]:
                    for (s, v) in ws:
                        if s.owner == ename:
                            continue
                        if waited.get(id(s), -1) >= v:
                            continue
                        eng.wait_ge(s.h, v)
                        waited[id(s)] = v
                    inst = fn(eng)
                    if inc is not None:
                        if inc.owner is None:
                            if inst is None:
                                inst = eng.nop()
                            inst.then_inc(inc.h, amt)
                        else:
                            eng.drain().then_inc(inc.h, amt)
                    elif ename in ("vector", "scalar", "gpsimd"):
                        eng.drain()
            return body

        for ename in self.ENGS:
            if self.ops[ename]:
                getattr(block, ename)(make_body(ename))
        return ctx


def tile_lhsT(W):
    OUT, K = W.shape
    arr = W.reshape(OUT // 128, 128, K // 128, 128)
    return np.ascontiguousarray(arr.transpose(0, 3, 2, 1))


def build_launch_A():
    nc = bass.Bass("TRN2", target_bir_lowering=False)
    x_in = nc.dram_tensor("x", [R, D], F32, kind="ExternalInput").ap()
    WI = nc.dram_tensor("WI", [NKD, 128, NKD, 128], F32, kind="ExternalInput").ap()
    WG = nc.dram_tensor("WG", [NG, 128, NKD, 128], F32, kind="ExternalInput").ap()
    bgT = nc.dram_tensor("bgT", [128, NG], F32, kind="ExternalInput").ap()
    consts = nc.dram_tensor("consts", [128, 8], F32, kind="ExternalInput").ap()

    sp_c = nc.dram_tensor("sp_c", [16, 128, R], BF16, kind="ExternalOutput").ap()
    sp_P = nc.dram_tensor("sp_P", [16, 128, R], BF16, kind="ExternalOutput").ap()
    sp_o = nc.dram_tensor("sp_o", [16, 128, R], BF16, kind="ExternalOutput").ap()
    exch = nc.dram_tensor("exch", [128, 32], F32, kind="ExternalOutput").ap()

    xln_hbm = nc.dram_tensor("xln_hbm", [NKD, R, 128], BF16).ap()

    r = Rec()
    sVE = r.sem("sVE", "vector")
    sSE = r.sem("sSE", "scalar")
    sPE = r.sem("sPE", "tensor")
    s_cst = r.sem("s_cst")
    s_x = r.dsems("s_x", 2)
    s_st = r.dsems("s_st", 2)
    s_tr = r.sem("s_tr")
    s_w = r.dsems("s_w", 2)
    s_spc = r.dsems("s_spc", 2)
    s_spP = r.dsems("s_spP", 2)
    s_spo = r.dsems("s_spo", 2)
    s_ex = r.sem("s_ex")

    st = contextlib.ExitStack()
    A = st.enter_context(nc.sbuf_tensor("A", [128, NKD * R], BF16))
    XP = st.enter_context(nc.sbuf_tensor("XP", [128, NKD * R], BF16))
    pool = [st.enter_context(nc.sbuf_tensor(f"T{i}", [128, R], F32)) for i in range(6)]
    cbuf = [st.enter_context(nc.sbuf_tensor(f"cb{i}", [128, R], BF16)) for i in range(2)]
    pbuf = [st.enter_context(nc.sbuf_tensor(f"pb{i}", [128, R], BF16)) for i in range(2)]
    wbuf = [st.enter_context(nc.sbuf_tensor(f"wb{i}", [128, NKD * 128], BF16)) for i in range(2)]
    cols = st.enter_context(nc.sbuf_tensor("cols", [128, 16], F32))
    cst = st.enter_context(nc.sbuf_tensor("cst", [128, 8], F32))
    bg_sb = st.enter_context(nc.sbuf_tensor("bg_sb", [128, NG], F32))
    exch_sb = st.enter_context(nc.sbuf_tensor("exch_sb", [128, 32], F32))
    wrm = st.enter_context(nc.sbuf_tensor("wrm", [128, 8], F32))
    ps_mm = [st.enter_context(nc.psum_tensor(f"ps{i}", [128, 512], F32)) for i in range(2)]

    r.add("sync", lambda e: e.dma_start(out=cst[:], in_=consts[:, :]), inc=s_cst, amt=16)
    tok_cst = r.add("sync", lambda e: e.dma_start(out=bg_sb[:], in_=bgT[:, :]), inc=s_cst, amt=16)
    r.add("scalar", lambda e: e.copy(wrm[:, 0:1], cst[:, 0:1]), waits=[tok_cst])
    r.add("scalar", lambda e: e.copy(wrm[:, 1:2], cst[:, 0:1]))
    r.add("vector", lambda e: e.memset(wrm[:, 2:3], 0.0))
    r.add("gpsimd", lambda e: e.memset(wrm[:, 4:5], 0.0))

    tok_apply = {}
    tok_store = {}
    for tt in range(NT):
        sl = tt % 2
        xb = pool[sl]
        stg = sl * R
        tok_x = r.add("sync", (lambda xb, tt: lambda e: e.dma_start(
            out=xb[:], in_=x_in[tt * 128:(tt + 1) * 128, :]))(xb, tt),
            waits=[tok_apply.get(tt - 2)], inc=s_x[sl], amt=16)
        scr = pool[2]
        r.add("vector", (lambda xb: lambda e: e.reduce_sum(cols[:, 0:1], xb[:], axis=AX.X))(xb),
              waits=[tok_x, tok_cst])
        r.add("vector", (lambda xb, scr: lambda e: e.tensor_mul(scr[:], xb[:], xb[:]))(xb, scr))
        r.add("vector", (lambda scr: lambda e: e.reduce_sum(cols[:, 1:2], scr[:], axis=AX.X))(scr))
        r.add("vector", lambda e: e.tensor_scalar(cols[:, 2:3], cols[:, 0:1], cst[:, C_INV2048:C_INV2048 + 1], None, ALU.mult))
        r.add("vector", lambda e: e.drain())
        r.add("vector", lambda e: e.tensor_mul(cols[:, 3:4], cols[:, 2:3], cols[:, 2:3]))
        r.add("vector", lambda e: e.drain())
        tok_v = r.add("vector", lambda e: e.tensor_scalar(cols[:, 4:5], cols[:, 1:2], cst[:, C_INV2048:C_INV2048 + 1],
                                                          cols[:, 3:4], ALU.mult, ALU.subtract), inc=sVE)
        r.add("scalar", lambda e: e.activation(cols[:, 5:6], cols[:, 4:5], AF.Ln, bias=cst[:, C_EPS_LN:C_EPS_LN + 1]),
              waits=[tok_v])
        r.add("scalar", lambda e: e.drain())
        tok_rstd = r.add("scalar", lambda e: e.activation(cols[:, 5:6], cols[:, 5:6], AF.Exp,
                                                          scale=cst[:, C_NEG_HALF:C_NEG_HALF + 1]), inc=sSE)
        tok_apply[tt] = r.add("vector", (lambda xb, stg: lambda e: e.tensor_scalar(
            A[:, stg:stg + D], xb[:], cols[:, 2:3], cols[:, 5:6], ALU.subtract, ALU.mult))(xb, stg),
            waits=[tok_rstd, tok_store.get(tt - 2)], inc=sVE)
        tok_w = None
        for dt in range(NKD):
            tok_w = r.add("sync", (lambda dt, tt, stg: lambda e: e.dma_start(
                out=xln_hbm[dt, tt * 128:(tt + 1) * 128, :],
                in_=A[:, stg + dt * 128:stg + (dt + 1) * 128]))(dt, tt, stg),
                waits=[tok_apply[tt]], inc=s_st[sl], amt=16)
        tok_store[tt] = tok_w

    tok_tr = None
    for dt in range(NKD):
        tok_tr = r.add("sync", (lambda dt: lambda e: e.dma_start(
            out=A[:, dt * R:(dt + 1) * R], in_=xln_hbm[dt, :, :], transpose=True))(dt),
            waits=[tok_store[NT - 2], tok_store[NT - 1]], inc=s_tr, amt=16)

    ev_tok = {}
    psum_i = [0]
    last_w_user = {}

    def emit_gemm_tile(weight_dram_block, rhs_sbuf, out_cb, extra_mm_waits=(), se_extra=()):
        wslot = (psum_i[0] // NCH) % 2
        wv = wbuf[wslot]
        tok_w = r.add("gpsimd", (lambda wv, weight_dram_block: lambda e: e.dma_start(
            out=wv.ap().rearrange("p (kt m) -> p kt m", m=128), in_=weight_dram_block))(wv, weight_dram_block),
            waits=[last_w_user.get(wslot)], inc=s_w[wslot], amt=16)
        last_e = None
        last_m = None
        for nch in range(NCH):
            i = psum_i[0]
            psum_i[0] += 1
            ps = ps_mm[i % 2]

            def mmgroup(e, wv=wv, ps=ps, nch=nch, rhs_sbuf=rhs_sbuf):
                inst = None
                for kt in range(NKD):
                    inst = e.matmul(ps[:, :], wv[:, kt * 128:(kt + 1) * 128],
                                    rhs_sbuf[:, kt * R + nch * 512: kt * R + nch * 512 + 512],
                                    start=(kt == 0), stop=(kt == NKD - 1))
                return inst
            tok_m = r.add("tensor", mmgroup,
                          waits=[tok_w, ev_tok.get(i - 2)] + list(extra_mm_waits), inc=sPE)
            last_m = tok_m
            eng, fn = out_cb(nch, ps)
            tok_e = r.add(eng, fn, waits=[tok_m] + list(se_extra), inc=(sSE if eng == "scalar" else sVE))
            ev_tok[i] = tok_e
            last_e = tok_e
        last_w_user[wslot] = last_m
        return last_e

    for ht in range(NKD):
        def mk_cb(ht):
            def cb(nch, ps):
                return ("scalar", (lambda ht, nch, ps: lambda e: e.copy(
                    XP[:, ht * R + nch * 512: ht * R + nch * 512 + 512], ps[:, :]))(ht, nch, ps))
            return cb
        emit_gemm_tile(WI[ht, :, :, :], A.ap(), mk_cb(ht), extra_mm_waits=[tok_tr])

    tok_xp_done = ev_tok[NKD * NCH - 1]

    def gate_cb(func, out_ap_fn, gt):
        def cb(nch, ps):
            return ("scalar", (lambda nch, ps: lambda e: e.activation(
                out_ap_fn(nch), ps[:, :], func, bias=bg_sb[:, gt:gt + 1]))(nch, ps))
        return cb

    tok_scan = {}
    tok_u = {}
    tok_spill = {}
    tok_exch = None
    for h in range(HEADS):
        for j in range(4):
            tok_i = emit_gemm_tile(WG[h * 16 + 0 * 4 + j, :, :, :], XP.ap(),
                                   gate_cb(AF.Exp, (lambda j: lambda nch: pool[j][:, nch * 512:(nch + 1) * 512])(j),
                                           h * 16 + 0 * 4 + j),
                                   extra_mm_waits=[tok_xp_done],
                                   se_extra=[tok_scan.get(("c", h - 1, j)), tok_cst])
            tok_ct = emit_gemm_tile(WG[h * 16 + 3 * 4 + j, :, :, :], XP.ap(),
                                    gate_cb(AF.Tanh, lambda nch: pool[4][:, nch * 512:(nch + 1) * 512],
                                            h * 16 + 3 * 4 + j),
                                    se_extra=[tok_u.get((h, j - 1)), tok_scan.get(("P", h - 1, 3))])
            tok_u[(h, j)] = r.add("vector", (lambda j: lambda e: e.tensor_mul(pool[j][:], pool[j][:], pool[4][:]))(j),
                                  waits=[tok_i, tok_ct], inc=sVE)
        tok_z = r.add("vector", lambda e: e.memset(pool[4][:], 0.0), waits=[tok_u[(h, 3)]], inc=sVE)
        for j in range(4):
            ct_idx = h * 4 + j
            w_f = [tok_scan.get(("P", h, j - 1)) if j > 0 else tok_scan.get(("P", h - 1, 3))]
            tok_f = emit_gemm_tile(WG[h * 16 + 1 * 4 + j, :, :, :], XP.ap(),
                                   gate_cb(AF.Sigmoid, lambda nch: pool[5][:, nch * 512:(nch + 1) * 512],
                                           h * 16 + 1 * 4 + j),
                                   se_extra=w_f)
            cb_t = cbuf[j % 2]
            pb_t = pbuf[j % 2]
            tok_c = r.add("vector", (lambda cb_t, j: lambda e: e.tensor_tensor_scan(
                cb_t[:], pool[5][:], pool[j][:], 0.0, ALU.mult, ALU.add))(cb_t, j),
                waits=[tok_f, tok_u[(h, j)], tok_spill.get(("c", ct_idx - 2))], inc=sVE)
            tok_scan[("c", h, j)] = tok_c
            r.add("vector", (lambda cb_t, ct_idx: lambda e: e.tensor_copy(
                exch_sb[:, ct_idx:ct_idx + 1], cb_t[:, R - 1:R]))(cb_t, ct_idx), inc=sVE)
            tok_P = r.add("vector", (lambda pb_t, j: lambda e: e.tensor_tensor_scan(
                pb_t[:], pool[5][:], pool[4][:], 1.0, ALU.mult, ALU.add))(pb_t, j),
                waits=[tok_z, tok_spill.get(("P", ct_idx - 2))], inc=sVE)
            tok_scan[("P", h, j)] = tok_P
            tok_exch = r.add("vector", (lambda pb_t, ct_idx: lambda e: e.tensor_copy(
                exch_sb[:, 16 + ct_idx:17 + ct_idx], pb_t[:, R - 1:R]))(pb_t, ct_idx), inc=sVE)
            tok_spill[("c", ct_idx)] = r.add("scalar", (lambda cb_t, ct_idx: lambda e: e.dma_start(
                out=sp_c[ct_idx, :, :], in_=cb_t[:]))(cb_t, ct_idx), waits=[tok_c], inc=s_spc[j % 2], amt=16)
            tok_spill[("P", ct_idx)] = r.add("scalar", (lambda pb_t, ct_idx: lambda e: e.dma_start(
                out=sp_P[ct_idx, :, :], in_=pb_t[:]))(pb_t, ct_idx), waits=[tok_P], inc=s_spP[j % 2], amt=16)
            ob_off = (ct_idx % 2) * R
            tok_o = emit_gemm_tile(WG[h * 16 + 2 * 4 + j, :, :, :], XP.ap(),
                                   gate_cb(AF.Sigmoid,
                                           (lambda ob_off: lambda nch: A[:, ob_off + nch * 512: ob_off + (nch + 1) * 512])(ob_off),
                                           h * 16 + 2 * 4 + j),
                                   se_extra=[tok_spill.get(("o", ct_idx - 2))])
            tok_spill[("o", ct_idx)] = r.add("scalar", (lambda ob_off, ct_idx: lambda e: e.dma_start(
                out=sp_o[ct_idx, :, :], in_=A[:, ob_off:ob_off + R]))(ob_off, ct_idx),
                waits=[tok_o], inc=s_spo[ct_idx % 2], amt=16)

    r.add("sync", lambda e: e.dma_start(out=exch[:, :], in_=exch_sb[:]),
          waits=[tok_exch], inc=s_ex, amt=16)
    r.add("sync", lambda e: e.nop(), waits=r.final_dma_waits())

    with nc.Block() as block:
        r.emit(nc, block)
    return nc


def build_launch_B():
    nc = bass.Bass("TRN2", target_bir_lowering=False)
    x_in = nc.dram_tensor("x", [R, D], F32, kind="ExternalInput").ap()
    sp_c = nc.dram_tensor("sp_c", [16, 128, R], BF16, kind="ExternalInput").ap()
    sp_P = nc.dram_tensor("sp_P", [16, 128, R], BF16, kind="ExternalInput").ap()
    sp_o = nc.dram_tensor("sp_o", [16, 128, R], BF16, kind="ExternalInput").ap()
    exch_all = nc.dram_tensor("exch_all", [NC * 128, 32], F32, kind="ExternalInput").ap()
    mks = nc.dram_tensor("mks", [128, 8], F32, kind="ExternalInput").ap()
    mks1 = nc.dram_tensor("mks1", [128, 8], F32, kind="ExternalInput").ap()
    rmsT = nc.dram_tensor("rmsT", [128, 4], F32, kind="ExternalInput").ap()
    WO = nc.dram_tensor("WO", [NKD, 128, D], F32, kind="ExternalInput").ap()
    WFG = nc.dram_tensor("WFG", [NF, 128, NKD, 128], F32, kind="ExternalInput").ap()
    WFU = nc.dram_tensor("WFU", [NF, 128, NKD, 128], F32, kind="ExternalInput").ap()
    WD = nc.dram_tensor("WD", [NF, 128, D], F32, kind="ExternalInput").ap()
    consts = nc.dram_tensor("consts", [128, 8], F32, kind="ExternalInput").ap()

    out = nc.dram_tensor("out", [R, D], F32, kind="ExternalOutput").ap()

    x2_hbm = nc.dram_tensor("x2_hbm", [R, D], F32).ap()
    x2ln_hbm = nc.dram_tensor("x2ln_hbm", [NKD, R, 128], BF16).ap()
    gated_hbm = nc.dram_tensor("gated_hbm", [NT, 128, NF, 128], BF16).ap()
    ffnout_hbm = nc.dram_tensor("ffnout_hbm", [R, D], F32).ap()

    r = Rec()
    sVE = r.sem("sVE", "vector")
    sSE = r.sem("sSE", "scalar")
    sPE = r.sem("sPE", "tensor")
    s_cst = r.sem("s_cst")
    s_lc = r.dsems("s_lc", 2)
    s_lp = r.dsems("s_lp", 2)
    s_lo = r.sem("s_lo")
    s_wo = r.sem("s_wo")
    s_x = r.dsems("s_x", 2)
    s_x2w = r.dsems("s_x2w", 2)
    s_x2lnw = r.sem("s_x2lnw")
    s_tr2 = r.sem("s_tr2")
    s_wg = r.dsems("s_wg", 2)
    s_wu = r.dsems("s_wu", 2)
    s_gst = r.dsems("s_gst", 2)
    s_wd = r.sem("s_wd")
    s_gt = r.dsems("s_gt", 2)
    s_ffw = r.dsems("s_ffw", 2)
    s_lf = r.dsems("s_lf", 2)
    s_lx2 = r.dsems("s_lx2", 2)
    s_outw = r.dsems("s_outw", 2)

    st = contextlib.ExitStack()
    A = st.enter_context(nc.sbuf_tensor("A", [128, NKD * R], BF16))
    BIG = st.enter_context(nc.sbuf_tensor("BIG", [128, NKD * R], BF16))
    pool = [st.enter_context(nc.sbuf_tensor(f"T{i}", [128, R], F32)) for i in range(6)]
    cbuf = [st.enter_context(nc.sbuf_tensor(f"cb{i}", [128, R], BF16)) for i in range(2)]
    pbuf = [st.enter_context(nc.sbuf_tensor(f"pb{i}", [128, R], BF16)) for i in range(2)]
    ob0 = st.enter_context(nc.sbuf_tensor("ob0", [128, R], BF16))
    obuf = [ob0, ob0]
    cols = st.enter_context(nc.sbuf_tensor("cols", [128, 16], F32))
    cst = st.enter_context(nc.sbuf_tensor("cst", [128, 8], F32))
    m_sb = st.enter_context(nc.sbuf_tensor("m_sb", [128, 8], F32))
    m1_sb = st.enter_context(nc.sbuf_tensor("m1_sb", [128, 8], F32))
    rms_sb = st.enter_context(nc.sbuf_tensor("rms_sb", [128, 4], F32))
    exa_sb = st.enter_context(nc.sbuf_tensor("exa_sb", [128, NC * 32], F32))
    cin_sb = st.enter_context(nc.sbuf_tensor("cin_sb", [128, 16], F32))
    ta_sb = st.enter_context(nc.sbuf_tensor("ta_sb", [128, 16], F32))
    tb_sb = st.enter_context(nc.sbuf_tensor("tb_sb", [128, 16], F32))
    ones_c = st.enter_context(nc.sbuf_tensor("ones_c", [128, 1], F32))
    ones_r = st.enter_context(nc.sbuf_tensor("ones_r", [1, 128], F32))
    row2 = st.enter_context(nc.sbuf_tensor("row2", [1, D], F32))
    wrm = st.enter_context(nc.sbuf_tensor("wrm", [128, 8], F32))
    PS_A = st.enter_context(nc.psum_tensor("PS_A", [128, D], F32))
    PS_B = st.enter_context(nc.psum_tensor("PS_B", [128, D], F32))

    r.add("sync", lambda e: e.dma_start(out=cst[:], in_=consts[:, :]), inc=s_cst, amt=16)
    r.add("sync", lambda e: e.dma_start(out=m_sb[:], in_=mks[:, :]), inc=s_cst, amt=16)
    r.add("sync", lambda e: e.dma_start(out=m1_sb[:], in_=mks1[:, :]), inc=s_cst, amt=16)
    r.add("sync", lambda e: e.dma_start(out=rms_sb[:], in_=rmsT[:, :]), inc=s_cst, amt=16)
    tok_cst = r.add("sync", lambda e: e.dma_start(
        out=exa_sb.ap().rearrange("p (r c) -> p r c", c=32),
        in_=exch_all.rearrange("(r p) c -> p r c", p=128)), inc=s_cst, amt=16)
    r.add("scalar", lambda e: e.copy(wrm[:, 0:1], cst[:, 0:1]), waits=[tok_cst])
    r.add("scalar", lambda e: e.copy(wrm[:, 1:2], cst[:, 0:1]))
    r.add("gpsimd", lambda e: e.memset(wrm[:, 4:5], 0.0))
    r.add("vector", lambda e: e.memset(cin_sb[:], 0.0))
    r.add("vector", lambda e: e.memset(ones_c[:], 1.0))
    tok_ones = r.add("vector", lambda e: e.memset(ones_r[:], 1.0), inc=sVE)
    r.add("vector", lambda e: e.nop(), waits=[tok_cst])
    for j in range(NC - 1):
        r.add("vector", (lambda j: lambda e: e.tensor_scalar(
            ta_sb[:], exa_sb[:, j * 32 + 16:j * 32 + 32], m_sb[:, j:j + 1], m1_sb[:, j:j + 1], ALU.mult, ALU.add))(j))
        r.add("vector", (lambda j: lambda e: e.tensor_scalar(
            tb_sb[:], exa_sb[:, j * 32:j * 32 + 16], m_sb[:, j:j + 1], None, ALU.mult))(j))
        r.add("vector", lambda e: e.drain())
        r.add("vector", lambda e: e.tensor_mul(cin_sb[:], cin_sb[:], ta_sb[:]))
        r.add("vector", lambda e: e.drain())
        r.add("vector", lambda e: e.tensor_add(cin_sb[:], cin_sb[:], tb_sb[:]))
        r.add("vector", lambda e: e.drain())
    tok_cin = r.add("vector", lambda e: e.nop(), inc=sVE)

    tok_h = {}
    tok_cn = {}
    tok_fix = {}
    tok_rmsln = None
    prev_psb_reader = None
    for h in range(HEADS):
        toks_sq = {}
        for j in range(4):
            ct_idx = h * 4 + j
            tok_lc = r.add("sync", (lambda cbt, ct_idx: lambda e: e.dma_start(
                out=cbt[:], in_=sp_c[ct_idx, :, :]))(cbuf[j % 2], ct_idx),
                waits=[tok_fix.get(ct_idx - 2)], inc=s_lc[j % 2], amt=16)
            tok_lp = r.add("sync", (lambda pbt, ct_idx: lambda e: e.dma_start(
                out=pbt[:], in_=sp_P[ct_idx, :, :]))(pbuf[j % 2], ct_idx), inc=s_lp[j % 2], amt=16)
            tok_fix[ct_idx] = r.add("vector", (lambda j, ct_idx: lambda e: e.scalar_tensor_tensor(
                pool[j][:], pbuf[j % 2][:], cin_sb[:, ct_idx:ct_idx + 1], cbuf[j % 2][:], ALU.mult, ALU.add))(j, ct_idx),
                waits=[tok_lc, tok_lp, tok_cin, tok_cn.get(ct_idx - 4)], inc=sVE)
            tok_sq = r.add("scalar", (lambda j: lambda e: e.activation(pool[4][:], pool[j][:], AF.Square))(j),
                           waits=[tok_fix[ct_idx], toks_sq.get(j - 1)], inc=sSE)

            def ssq(e, j=j):
                inst = None
                for nch in range(NCH):
                    inst = e.matmul(PS_A[0:1, nch * 512:(nch + 1) * 512], ones_c[:, :],
                                    pool[4][:, nch * 512:(nch + 1) * 512],
                                    start=(j == 0), stop=(j == 3))
                return inst
            toks_sq[j] = r.add("tensor", ssq, waits=[tok_sq, tok_rmsln if j == 0 else None,
                                                      tok_ones if ct_idx == 0 else None], inc=sPE)
        tok_bc = None
        for nch in range(NCH):
            r.add("scalar", (lambda nch: lambda e: e.activation(
                rmsrow[:], PS_A[0:1, nch * 512:(nch + 1) * 512], AF.Ln,
                scale=cst[0:1, C_INV512:C_INV512 + 1],
                bias=cst[0:1, C_EPS_RMS:C_EPS_RMS + 1]))(nch),
                waits=[toks_sq[3], tok_bc])
            r.add("scalar", lambda e: e.drain())
            tok_inv = r.add("scalar", lambda e: e.activation(rmsrow[:], rmsrow[:], AF.Exp,
                                                             scale=cst[0:1, C_NEG_HALF:C_NEG_HALF + 1]), inc=sSE)
            tok_bc = r.add("tensor", (lambda nch: lambda e: e.matmul(
                PS_B[:, nch * 512:(nch + 1) * 512], ones_r[:, :], rmsrow[0:1, :],
                start=True, stop=True))(nch),
                waits=[tok_inv, prev_psb_reader if nch == 0 else None], inc=sPE)
        tok_rmsln = tok_inv
        for j in range(4):
            ct_idx = h * 4 + j
            tok_lo = r.add("sync", (lambda obt, ct_idx: lambda e: e.dma_start(
                out=obt[:], in_=sp_o[ct_idx, :, :]))(obuf[j % 2], ct_idx),
                waits=[tok_h.get(ct_idx - 1)], inc=s_lo, amt=16)
            tok_cn[ct_idx] = r.add("vector", (lambda j: lambda e: e.scalar_tensor_tensor(
                pool[4][:], pool[j][:], rms_sb[:, j:j + 1], PS_B[:, :], ALU.mult, ALU.mult))(j),
                waits=[tok_bc], inc=sVE)
            tok_th = r.add("scalar", lambda e: e.activation(pool[5][:], pool[4][:], AF.Tanh),
                           waits=[tok_cn[ct_idx]], inc=sSE)
            tok_h[ct_idx] = r.add("vector", (lambda ct_idx, j: lambda e: e.tensor_mul(
                A[:, ct_idx * R:(ct_idx + 1) * R], pool[5][:], obuf[j % 2][:]))(ct_idx, j),
                waits=[tok_th, tok_lo], inc=sVE)
        prev_psb_reader = tok_cn[h * 4 + 3]

    tok_wo = None
    for kt in range(NKD):
        tok_wo = r.add("gpsimd", (lambda kt: lambda e: e.dma_start(
            out=BIG[:, kt * R:kt * R + D], in_=WO[kt, :, :]))(kt), inc=s_wo, amt=16)
    tok_att_ev = {}
    tok_x2w = {}
    tok_x2lnw = {}
    tok_x2 = {}
    tok_att = None
    for tt in range(NT):
        ps = PS_A if tt % 2 == 0 else PS_B

        def attmm(e, tt=tt, ps=ps):
            inst = None
            for kt in range(NKD):
                for dch in range(NCH):
                    inst = e.matmul(ps[:, dch * 512:(dch + 1) * 512],
                                    A[:, kt * R + tt * 128: kt * R + (tt + 1) * 128],
                                    BIG[:, kt * R + dch * 512: kt * R + dch * 512 + 512],
                                    start=(kt == 0), stop=(kt == NKD - 1))
            return inst
        first_guard = None
        if tt == 0:
            first_guard = tok_rmsln
        elif tt == 1:
            first_guard = tok_cn[15]
        tok_att = r.add("tensor", attmm,
                        waits=[tok_h[15], tok_wo, tok_att_ev.get(tt - 2), first_guard], inc=sPE)
        att = pool[tt % 2]
        xb = pool[2 + tt % 2]
        x2 = pool[4 + tt % 2]
        tok_att_ev[tt] = r.add("vector", (lambda att, ps: lambda e: e.tensor_copy(att[:], ps[:, :]))(att, ps),
                               waits=[tok_att], inc=sVE)
        tok_x = r.add("sync", (lambda xb, tt: lambda e: e.dma_start(
            out=xb[:], in_=x_in[tt * 128:(tt + 1) * 128, :]))(xb, tt),
            waits=[tok_x2.get(tt - 2),
                   tok_cn.get(14) if tt == 0 else None,
                   tok_cn.get(15) if tt == 1 else None], inc=s_x[tt % 2], amt=16)
        r.add("vector", (lambda att: lambda e: e.reduce_sum(cols[:, 0:1], att[:], axis=AX.X))(att))
        r.add("vector", (lambda att, x2: lambda e: e.tensor_mul(x2[:], att[:], att[:]))(att, x2),
              waits=[tok_x2w.get(tt - 2)])
        r.add("vector", (lambda x2: lambda e: e.reduce_sum(cols[:, 1:2], x2[:], axis=AX.X))(x2))
        r.add("vector", lambda e: e.tensor_scalar(cols[:, 2:3], cols[:, 0:1], cst[:, C_INV2048:C_INV2048 + 1], None, ALU.mult))
        r.add("vector", lambda e: e.drain())
        r.add("vector", lambda e: e.tensor_mul(cols[:, 3:4], cols[:, 2:3], cols[:, 2:3]))
        r.add("vector", lambda e: e.drain())
        tok_v = r.add("vector", lambda e: e.tensor_scalar(cols[:, 4:5], cols[:, 1:2], cst[:, C_INV2048:C_INV2048 + 1],
                                                          cols[:, 3:4], ALU.mult, ALU.subtract), inc=sVE)
        r.add("scalar", lambda e: e.activation(cols[:, 5:6], cols[:, 4:5], AF.Ln, bias=cst[:, C_EPS_LN:C_EPS_LN + 1]),
              waits=[tok_v])
        r.add("scalar", lambda e: e.drain())
        tok_rstd = r.add("scalar", lambda e: e.activation(cols[:, 5:6], cols[:, 5:6], AF.Exp,
                                                          scale=cst[:, C_NEG_HALF:C_NEG_HALF + 1]), inc=sSE)
        r.add("vector", (lambda att: lambda e: e.tensor_scalar(att[:], att[:], cols[:, 2:3], cols[:, 5:6],
                                                               ALU.subtract, ALU.mult))(att), waits=[tok_rstd])
        tok_x2[tt] = r.add("vector", (lambda x2, att, xb: lambda e: e.tensor_add(x2[:], att[:], xb[:]))(x2, att, xb),
                           waits=[tok_x], inc=sVE)
        r.add("vector", (lambda x2: lambda e: e.reduce_sum(cols[:, 8:9], x2[:], axis=AX.X))(x2))
        r.add("vector", (lambda att, x2: lambda e: e.tensor_mul(att[:], x2[:], x2[:]))(att, x2))
        r.add("vector", (lambda att: lambda e: e.reduce_sum(cols[:, 9:10], att[:], axis=AX.X))(att))
        r.add("vector", lambda e: e.tensor_scalar(cols[:, 10:11], cols[:, 8:9], cst[:, C_INV2048:C_INV2048 + 1], None, ALU.mult))
        r.add("vector", lambda e: e.drain())
        r.add("vector", lambda e: e.tensor_mul(cols[:, 11:12], cols[:, 10:11], cols[:, 10:11]))
        r.add("vector", lambda e: e.drain())
        tok_v2 = r.add("vector", lambda e: e.tensor_scalar(cols[:, 12:13], cols[:, 9:10], cst[:, C_INV2048:C_INV2048 + 1],
                                                           cols[:, 11:12], ALU.mult, ALU.subtract), inc=sVE)
        r.add("scalar", lambda e: e.activation(cols[:, 13:14], cols[:, 12:13], AF.Ln, bias=cst[:, C_EPS_LN:C_EPS_LN + 1]),
              waits=[tok_v2])
        r.add("scalar", lambda e: e.drain())
        tok_rstd2 = r.add("scalar", lambda e: e.activation(cols[:, 13:14], cols[:, 13:14], AF.Exp,
                                                           scale=cst[:, C_NEG_HALF:C_NEG_HALF + 1]), inc=sSE)
        tok_x2ln = r.add("vector", (lambda tt, x2: lambda e: e.tensor_scalar(
            obuf[tt % 2][:], x2[:], cols[:, 10:11], cols[:, 13:14], ALU.subtract, ALU.mult))(tt, x2),
            waits=[tok_rstd2, tok_x2lnw.get(tt - 1), tok_h.get(15) if tt == 0 else None], inc=sVE)
        tok_x2w[tt] = r.add("scalar", (lambda x2, tt: lambda e: e.dma_start(
            out=x2_hbm[tt * 128:(tt + 1) * 128, :], in_=x2[:]))(x2, tt), waits=[tok_x2[tt]], inc=s_x2w[tt % 2], amt=16)
        last = None
        for dt in range(NKD):
            last = r.add("scalar", (lambda dt, tt: lambda e: e.dma_start(
                out=x2ln_hbm[dt, tt * 128:(tt + 1) * 128, :], in_=obuf[tt % 2][:, dt * 128:(dt + 1) * 128]))(dt, tt),
                waits=[tok_x2ln], inc=s_x2lnw, amt=16)
        tok_x2lnw[tt] = last

    tok_tr2 = None
    for dt in range(NKD):
        tok_tr2 = r.add("sync", (lambda dt: lambda e: e.dma_start(
            out=A[:, dt * R:(dt + 1) * R], in_=x2ln_hbm[dt, :, :], transpose=True))(dt),
            waits=[tok_x2lnw[NT - 1], tok_att if dt == 0 else None], inc=s_tr2, amt=16)

    wslot = [BIG.ap()[:, i * D:(i + 1) * D] for i in range(4)]
    gslot = [BIG.ap()[:, (4 + i) * D:(5 + i) * D] for i in range(2)]
    tok_pair = {}
    tok_gst = {}
    last_pair = None
    last_silu = None
    for ft in range(NF):
        wg_v = wslot[(ft % 2) * 2 + 0]
        wu_v = wslot[(ft % 2) * 2 + 1]
        tok_wg = r.add("gpsimd", (lambda ft, wg_v: lambda e: e.dma_start(
            out=wg_v.rearrange("p (kt m) -> p kt m", m=128), in_=WFG[ft, :, :, :]))(ft, wg_v),
            waits=[tok_pair.get((ft - 2, 3)), tok_att if ft == 0 else None], inc=s_wg[ft % 2], amt=16)
        tok_wu = r.add("gpsimd", (lambda ft, wu_v: lambda e: e.dma_start(
            out=wu_v.rearrange("p (kt m) -> p kt m", m=128), in_=WFU[ft, :, :, :]))(ft, wu_v),
            inc=s_wu[ft % 2], amt=16)
        gs = gslot[ft % 2]
        for nch in range(NCH):
            slot = (nch % 2) * 2
            psg = PS_A[:, slot * 512:(slot + 1) * 512]
            psu = PS_A[:, (slot + 1) * 512:(slot + 2) * 512]

            def gmm(e, wv=wg_v, psx=psg, nch=nch):
                inst = None
                for kt in range(NKD):
                    inst = e.matmul(psx, wv[:, kt * 128:(kt + 1) * 128],
                                    A[:, kt * R + nch * 512: kt * R + nch * 512 + 512],
                                    start=(kt == 0), stop=(kt == NKD - 1))
                return inst

            def umm(e, wv=wu_v, psx=psu, nch=nch):
                inst = None
                for kt in range(NKD):
                    inst = e.matmul(psx, wv[:, kt * 128:(kt + 1) * 128],
                                    A[:, kt * R + nch * 512: kt * R + nch * 512 + 512],
                                    start=(kt == 0), stop=(kt == NKD - 1))
                return inst
            prev = tok_pair.get((ft - 1, nch + 2) if nch < 2 else (ft, nch - 2))
            guard0 = None
            guard1 = tok_att_ev.get(NT - 2) if (ft == 0 and nch == 0) else None
            tok_g = r.add("tensor", gmm, waits=[tok_wg, tok_tr2, prev, guard0, guard1], inc=sPE)
            tok_um = r.add("tensor", umm, waits=[tok_wu], inc=sPE)
            tok_sg = r.add("scalar", (lambda psg: lambda e: e.activation(pool[0][:, 0:512], psg, AF.Silu))(psg),
                           waits=[tok_g, last_pair], inc=sSE)
            last_silu = tok_sg
            tok_pair[(ft, nch)] = r.add("vector", (lambda gs, nch, psu: lambda e: e.tensor_tensor(
                gs[:, nch * 512:(nch + 1) * 512], pool[0][:, 0:512], psu, ALU.mult))(gs, nch, psu),
                waits=[tok_sg, tok_um, tok_gst.get(ft - 2)], inc=sVE)
            last_pair = tok_pair[(ft, nch)]
        last = None
        for tt in range(NT):
            last = r.add("scalar", (lambda tt, ft, gs: lambda e: e.dma_start(
                out=gated_hbm[tt, :, ft, :], in_=gs[:, tt * 128:(tt + 1) * 128]))(tt, ft, gs),
                waits=[tok_pair[(ft, 3)]], inc=s_gst[ft % 2], amt=16)
        tok_gst[ft] = last

    gt_sl = [A.ap()[:, 0:NF * 128], A.ap()[:, NF * 128:2 * NF * 128]]
    tok_dn_ev = {}
    tok_ffw = {}
    last_gt_user = {}
    for dq in range(4):
        tok_wd = None
        for kt in range(NF):
            tok_wd = r.add("gpsimd", (lambda dq, kt: lambda e: e.dma_start(
                out=BIG[:, kt * 512:(kt + 1) * 512], in_=WD[kt, :, dq * 512:(dq + 1) * 512]))(dq, kt),
                waits=[last_pair if (dq == 0 and kt == 0) else None,
                       tok_gst.get(NF - 1) if (dq == 0 and kt == 0) else None,
                       tok_gst.get(NF - 2) if (dq == 0 and kt == 0) else None,
                       tok_dn_ev.get((dq - 1, NT - 1)) if kt == 0 else None,
                       tok_dn_ev.get((dq - 1, NT - 2)) if kt == 0 else None], inc=s_wd, amt=16)
        for tt in range(NT):
            gv = gt_sl[tt % 2]
            tok_gtl = r.add("sync", (lambda tt, gv: lambda e: e.dma_start(
                out=gv.rearrange("p (f c) -> p f c", c=128), in_=gated_hbm[tt, :, :, :]))(tt, gv),
                waits=[tok_gst.get(NF - 1) if (dq == 0 and tt == 0) else None,
                       tok_gst.get(NF - 2) if (dq == 0 and tt == 0) else None,
                       last_pair if (dq == 0 and tt < 2) else None,
                       last_gt_user.get(tt % 2)], inc=s_gt[tt % 2], amt=16)
            ps = PS_B[:, (tt % 2) * 512:(tt % 2 + 1) * 512]

            def dnmm(e, gv=gv, psx=ps):
                inst = None
                for kt in range(NF):
                    inst = e.matmul(psx, gv[:, kt * 128:(kt + 1) * 128],
                                    BIG[:, kt * 512:(kt + 1) * 512],
                                    start=(kt == 0), stop=(kt == NF - 1))
                return inst
            tok_dn = r.add("tensor", dnmm,
                           waits=[tok_gtl, tok_wd, tok_dn_ev.get((dq, tt - 2)),
                                  tok_dn_ev.get((dq - 1, NT - 2 + (tt % 2))) if tt < 2 else None],
                           inc=sPE)
            last_gt_user[tt % 2] = tok_dn
            ev = pool[tt % 2]
            tok_ev = r.add("vector", (lambda ev, psx: lambda e: e.tensor_copy(ev[:, 0:512], psx))(ev, ps),
                           waits=[tok_dn, tok_ffw.get((dq, tt - 2)),
                                  tok_ffw.get((dq - 1, NT - 2 + (tt % 2))) if tt < 2 else None,
                                  last_silu if (dq == 0 and tt < 2) else None],
                           inc=sVE)
            tok_dn_ev[(dq, tt)] = tok_ev
            tok_ffw[(dq, tt)] = r.add("scalar", (lambda ev, tt, dq: lambda e: e.dma_start(
                out=ffnout_hbm[tt * 128:(tt + 1) * 128, dq * 512:(dq + 1) * 512], in_=ev[:, 0:512]))(ev, tt, dq),
                waits=[tok_ev], inc=s_ffw[tt % 2], amt=16)

    tok_outw = {}
    tok_ot = {}
    for tt in range(NT):
        ffn = pool[tt % 2]
        x2 = pool[2 + tt % 2]
        ot = pool[4 + tt % 2]
        ffw_done = ([tok_ffw[(dq, tp)] for dq in range(4) for tp in (NT - 2, NT - 1)]
                    if tt == 0 else [])
        tok_lf = r.add("sync", (lambda ffn, tt: lambda e: e.dma_start(
            out=ffn[:], in_=ffnout_hbm[tt * 128:(tt + 1) * 128, :]))(ffn, tt),
            waits=ffw_done + [tok_outw.get(tt - 2),
                              tok_dn_ev.get((3, NT - 2 + (tt % 2))) if tt < 2 else None],
            inc=s_lf[tt % 2], amt=16)
        tok_lx2 = r.add("sync", (lambda x2, tt: lambda e: e.dma_start(
            out=x2[:], in_=x2_hbm[tt * 128:(tt + 1) * 128, :]))(x2, tt), inc=s_lx2[tt % 2], amt=16)
        r.add("vector", (lambda ffn: lambda e: e.reduce_sum(cols[:, 0:1], ffn[:], axis=AX.X))(ffn),
              waits=[tok_lf])
        r.add("vector", (lambda ffn, ot: lambda e: e.tensor_mul(ot[:], ffn[:], ffn[:]))(ffn, ot),
              waits=[tok_outw.get(tt - 2),
                     tok_x2w.get(NT - 2 + (tt % 2)) if tt < 2 else None])
        r.add("vector", (lambda ot: lambda e: e.reduce_sum(cols[:, 1:2], ot[:], axis=AX.X))(ot))
        r.add("vector", lambda e: e.tensor_scalar(cols[:, 2:3], cols[:, 0:1], cst[:, C_INV2048:C_INV2048 + 1], None, ALU.mult))
        r.add("vector", lambda e: e.drain())
        r.add("vector", lambda e: e.tensor_mul(cols[:, 3:4], cols[:, 2:3], cols[:, 2:3]))
        r.add("vector", lambda e: e.drain())
        tok_v = r.add("vector", lambda e: e.tensor_scalar(cols[:, 4:5], cols[:, 1:2], cst[:, C_INV2048:C_INV2048 + 1],
                                                          cols[:, 3:4], ALU.mult, ALU.subtract), inc=sVE)
        r.add("scalar", lambda e: e.activation(cols[:, 5:6], cols[:, 4:5], AF.Ln, bias=cst[:, C_EPS_LN:C_EPS_LN + 1]),
              waits=[tok_v])
        r.add("scalar", lambda e: e.drain())
        tok_rstd = r.add("scalar", lambda e: e.activation(cols[:, 5:6], cols[:, 5:6], AF.Exp,
                                                          scale=cst[:, C_NEG_HALF:C_NEG_HALF + 1]), inc=sSE)
        r.add("vector", (lambda ffn: lambda e: e.tensor_scalar(ffn[:], ffn[:], cols[:, 2:3], cols[:, 5:6],
                                                               ALU.subtract, ALU.mult))(ffn), waits=[tok_rstd])
        tok_ot[tt] = r.add("vector", (lambda ot, ffn, x2: lambda e: e.tensor_add(ot[:], ffn[:], x2[:]))(ot, ffn, x2),
                           waits=[tok_lx2], inc=sVE)
        tok_outw[tt] = r.add("sync", (lambda ot, tt: lambda e: e.dma_start(
            out=out[tt * 128:(tt + 1) * 128, :], in_=ot[:]))(ot, tt), waits=[tok_ot[tt]], inc=s_outw[tt % 2], amt=16)

    r.add("sync", lambda e: e.nop(), waits=r.final_dma_waits())

    with nc.Block() as block:
        r.emit(nc, block)
    return nc



def build_fused():
    nc = bass.Bass("TRN2", target_bir_lowering=False)
    xf = nc.dram_tensor("xf", [S, D], F32, kind="ExternalInput").ap()
    x_in = nc.dram_tensor("x", [R, D], F32, kind="ExternalInput").ap()
    WI = nc.dram_tensor("WI", [NKD, 128, NKD, 128], F32, kind="ExternalInput").ap()
    WG = nc.dram_tensor("WG", [NG, 128, NKD, 128], F32, kind="ExternalInput").ap()
    bgT = nc.dram_tensor("bgT", [128, NG], F32, kind="ExternalInput").ap()
    mselt = nc.dram_tensor("mselt", [128, 8], F32, kind="ExternalInput").ap()
    rmsT = nc.dram_tensor("rmsT", [128, 4], F32, kind="ExternalInput").ap()
    WO = nc.dram_tensor("WO", [NKD, 128, D], F32, kind="ExternalInput").ap()
    WFG = nc.dram_tensor("WFG", [NF, 128, NKD, 128], F32, kind="ExternalInput").ap()
    WFU = nc.dram_tensor("WFU", [NF, 128, NKD, 128], F32, kind="ExternalInput").ap()
    WD = nc.dram_tensor("WD", [NF, 128, D], F32, kind="ExternalInput").ap()
    consts = nc.dram_tensor("consts", [128, 8], F32, kind="ExternalInput").ap()
    out = nc.dram_tensor("out", [R, D], F32, kind="ExternalOutput").ap()

    xlnf_hbm = nc.dram_tensor("xlnf_hbm", [NKD, 7 * R, 128], BF16).ap()
    xlno_hbm = nc.dram_tensor("xlno_hbm", [NKD, R, 128], BF16).ap()
    sp_o = nc.dram_tensor("sp_o", [16, 128, R], BF16).ap()
    x2_hbm = nc.dram_tensor("x2_hbm", [R, D], F32).ap()
    x2ln_hbm = nc.dram_tensor("x2ln_hbm", [NKD, R, 128], BF16).ap()
    gated_hbm = nc.dram_tensor("gated_hbm", [NT, 128, NF, 128], BF16).ap()
    ffnout_hbm = nc.dram_tensor("ffnout_hbm", [R, D], F32).ap()

    r = Rec()
    sVE = r.sem("sVE", "vector")
    sSE = r.sem("sSE", "scalar")
    sPE = r.sem("sPE", "tensor")
    s_cst = r.sem("s_cst")
    s_sx = r.sem("s_sx")
    s_ax = r.sem("s_ax")
    s_st = r.dsems("s_st", 2)
    s_tr = r.sem("s_tr")
    s_w = r.dsems("s_w", 2)
    s_spo = r.dsems("s_spo", 2)
    s_lo = r.dsems("s_lo", 2)
    s_wo = r.sem("s_wo")
    s_x = r.dsems("s_x", 2)
    s_x2w = r.dsems("s_x2w", 2)
    s_x2lnw = r.sem("s_x2lnw")
    s_tr2 = r.sem("s_tr2")
    s_wg = r.dsems("s_wg", 2)
    s_wu = r.dsems("s_wu", 2)
    s_gst = r.dsems("s_gst", 2)
    s_wd = r.dsems("s_wd", 2)
    s_gt = r.dsems("s_gt", 2)
    s_ffw = r.dsems("s_ffw", 2)
    s_lf = r.dsems("s_lf", 2)
    s_lx2 = r.dsems("s_lx2", 2)
    s_outw = r.dsems("s_outw", 2)

    st = contextlib.ExitStack()
    A = st.enter_context(nc.sbuf_tensor("A", [128, NKD * R], BF16))
    XB = st.enter_context(nc.sbuf_tensor("XB", [128, NKD * R], BF16))
    pool = [st.enter_context(nc.sbuf_tensor(f"T{i}", [128, R], F32)) for i in range(7)]
    cbuf = [st.enter_context(nc.sbuf_tensor(f"cb{i}", [128, R], BF16)) for i in range(2)]
    obuf0 = st.enter_context(nc.sbuf_tensor("ob0", [128, R], BF16))
    obuf = [obuf0, obuf0]
    wbuf = [st.enter_context(nc.sbuf_tensor(f"wb{i}", [128, NKD * 128], BF16)) for i in range(2)]
    statsb = st.enter_context(nc.sbuf_tensor("statsb", [128, 256], F32))
    state = st.enter_context(nc.sbuf_tensor("state", [128, 16], F32))
    cand = st.enter_context(nc.sbuf_tensor("cand", [128, 112], BF16))
    cin_sb = st.enter_context(nc.sbuf_tensor("cin_sb", [128, 16], F32))
    ta_sb = st.enter_context(nc.sbuf_tensor("ta_sb", [128, 16], F32))
    cols = st.enter_context(nc.sbuf_tensor("cols", [128, 16], F32))
    cst = st.enter_context(nc.sbuf_tensor("cst", [128, 8], F32))
    bg_sb = st.enter_context(nc.sbuf_tensor("bg_sb", [128, NG], F32))
    m_sb = st.enter_context(nc.sbuf_tensor("m_sb", [128, 8], F32))
    rms_sb = st.enter_context(nc.sbuf_tensor("rms_sb", [128, 4], F32))
    ones_c = st.enter_context(nc.sbuf_tensor("ones_c", [128, 1], F32))
    ones_r = st.enter_context(nc.sbuf_tensor("ones_r", [1, 128], BF16))
    rmsrow = st.enter_context(nc.sbuf_tensor("rmsrow", [1, 512], BF16))
    wrm = st.enter_context(nc.sbuf_tensor("wrm", [128, 8], F32))
    PS_A = st.enter_context(nc.psum_tensor("PS_A", [128, D], F32))
    PS_B = st.enter_context(nc.psum_tensor("PS_B", [128, D], F32))
    ps_mm = [PS_A.ap()[:, 0:512], PS_A.ap()[:, 512:1024]]
    sxb = A.ap()[:, 14 * R:15 * R]
    dump = A.ap()[:, 15 * R:16 * R]

    r.add("sync", lambda e: e.dma_start(out=cst[:], in_=consts[:, :]), inc=s_cst, amt=16)
    r.add("sync", lambda e: e.dma_start(out=bg_sb[:], in_=bgT[:, :]), inc=s_cst, amt=16)
    r.add("sync", lambda e: e.dma_start(out=m_sb[:], in_=mselt[:, :]), inc=s_cst, amt=16)
    tok_cst = r.add("sync", lambda e: e.dma_start(out=rms_sb[:], in_=rmsT[:, :]), inc=s_cst, amt=16)
    r.add("scalar", lambda e: e.copy(wrm[:, 0:1], cst[:, 0:1]), waits=[tok_cst])
    r.add("scalar", lambda e: e.copy(wrm[:, 1:2], cst[:, 0:1]))
    r.add("vector", lambda e: e.memset(wrm[:, 2:3], 0.0))
    r.add("gpsimd", lambda e: e.memset(wrm[:, 4:5], 0.0))
    r.add("vector", lambda e: e.memset(ones_c[:], 1.0))
    r.add("vector", lambda e: e.memset(ones_r[:], 1.0))
    r.add("vector", lambda e: e.memset(state[:], 0.0))
    tok_init = r.add("vector", lambda e: e.memset(cin_sb[:], 0.0), inc=sVE)

    def src_tile(gt):
        return xf[gt * 128:(gt + 1) * 128, :] if gt < 112 else x_in[(gt - 112) * 128:(gt - 111) * 128, :]

    tok_wave = {}
    tok_statdone = {}
    st_ve = [None]
    st_se = [None]
    tok_apl = [None]
    tok_store = {}

    def emit_stats_tile(sblk, tt):
        gt = sblk * 16 + tt
        tok_l = r.add("gpsimd", (lambda gt: lambda e: e.dma_start(out=sxb, in_=src_tile(gt)))(gt),
                      waits=[st_ve[0], st_se[0], tok_cst if gt == 0 else None], inc=s_sx, amt=16)
        st_se[0] = r.add("scalar", (lambda gt: lambda e: e.activation(
            obuf0[:], sxb, AF.Square, accum_out=statsb[:, 128 + gt:129 + gt]))(gt),
            waits=[tok_l], inc=sSE)
        st_ve[0] = r.add("vector", (lambda gt: lambda e: e.reduce_sum(
            statsb[:, gt:gt + 1], sxb, axis=AX.X))(gt), waits=[tok_l], inc=sVE)

    def emit_wave(sblk, q):
        c0 = sblk * 16 + 4 * q
        r.add("vector", (lambda c0: lambda e: e.tensor_scalar(
            statsb[:, c0:c0 + 4], statsb[:, c0:c0 + 4], cst[:, C_INV2048:C_INV2048 + 1], None, ALU.mult))(c0),
            waits=[st_se[0]])
        r.add("vector", (lambda c0: lambda e: e.tensor_mul(
            ta_sb[:, 0:4], statsb[:, c0:c0 + 4], statsb[:, c0:c0 + 4]))(c0))
        r.add("vector", (lambda c0: lambda e: e.tensor_scalar(
            statsb[:, 128 + c0:132 + c0], statsb[:, 128 + c0:132 + c0],
            cst[:, C_INV2048:C_INV2048 + 1], None, ALU.mult))(c0))
        tok_vv = r.add("vector", (lambda c0: lambda e: e.tensor_sub(
            statsb[:, 128 + c0:132 + c0], statsb[:, 128 + c0:132 + c0], ta_sb[:, 0:4]))(c0), inc=sVE)
        r.add("scalar", (lambda c0: lambda e: e.activation(
            statsb[:, 128 + c0:132 + c0], statsb[:, 128 + c0:132 + c0], AF.Ln,
            bias=cst[:, C_EPS_LN:C_EPS_LN + 1]))(c0), waits=[tok_vv])
        r.add("scalar", lambda e: e.drain())
        tok_wave[(sblk, q)] = r.add("scalar", (lambda c0: lambda e: e.activation(
            statsb[:, 128 + c0:132 + c0], statsb[:, 128 + c0:132 + c0], AF.Exp,
            scale=cst[:, C_NEG_HALF:C_NEG_HALF + 1]))(c0), inc=sSE)
        tok_statdone[sblk] = (st_se[0], st_ve[0])

    def emit_apply_tile(sblk, tt):
        if True:
            gt = sblk * 16 + tt
            sl = gt % 2
            tok_l = r.add("sync", (lambda gt: lambda e: e.dma_start(out=pool[6][:], in_=src_tile(gt)))(gt),
                          waits=[tok_apl[0]], inc=s_ax, amt=16)
            tok_apl[0] = r.add("vector", (lambda gt, sl: lambda e: e.tensor_scalar(
                cbuf[sl][:], pool[6][:], statsb[:, gt:gt + 1], statsb[:, 128 + gt:129 + gt],
                ALU.subtract, ALU.mult))(gt, sl),
                waits=[tok_l, tok_wave[(sblk, (gt % 16) // 4)], tok_store.get(gt - 2)], inc=sVE)
            tok_w = None
            for dt in range(NKD):
                if gt < 112:
                    dst = xlnf_hbm[dt, gt * 128:(gt + 1) * 128, :]
                else:
                    dst = xlno_hbm[dt, (gt - 112) * 128:(gt - 111) * 128, :]
                tok_w = r.add("sync", (lambda dst, sl, dt: lambda e: e.dma_start(
                    out=dst, in_=cbuf[sl][:, dt * 128:(dt + 1) * 128]))(dst, sl, dt),
                    waits=[tok_apl[0]], inc=s_st[sl], amt=16)
            tok_store[gt] = tok_w

    ev_tok = {}
    psum_i = [0]
    last_w_user = {}

    tile_i = [0]

    def emit_gemm_tile(weight_dram_block, rhs_sbuf, out_cb, extra_mm_waits=(), se_extra=(),
                       fast=True):
        wslot = tile_i[0] % 2
        wv = wbuf[wslot]
        tok_w = r.add("gpsimd", (lambda wv, weight_dram_block: lambda e: e.dma_start(
            out=wv.ap().rearrange("p (kt m) -> p kt m", m=128), in_=weight_dram_block))(wv, weight_dram_block),
            waits=[last_w_user.get(wslot)], inc=s_w[wslot], amt=16)
        last_e = None
        last_m = None
        if fast:
            ti = tile_i[0]
            tile_i[0] += 1
            psf = PS_A if ti % 2 == 0 else PS_B

            def mmall(e, wv=wv, psf=psf, rhs_sbuf=rhs_sbuf):
                inst = None
                for kt in range(NKD):
                    for nch in range(NCH):
                        inst = e.matmul(psf[:, nch * 512:(nch + 1) * 512],
                                        wv[:, kt * 128:(kt + 1) * 128],
                                        rhs_sbuf[:, kt * R + nch * 512: kt * R + nch * 512 + 512],
                                        start=(kt == 0), stop=(kt == NKD - 1))
                return inst
            tok_m = r.add("tensor", mmall,
                          waits=[tok_w, ev_tok.get(("t", ti - 2))] + list(extra_mm_waits), inc=sPE)
            last_m = tok_m
            for nch in range(NCH):
                eng, fn = out_cb(nch, psf[:, nch * 512:(nch + 1) * 512])
                tok_e = r.add(eng, fn, waits=[tok_m] + list(se_extra),
                              inc=(sSE if eng == "scalar" else sVE))
                last_e = tok_e
            ev_tok[("t", ti)] = last_e
            psum_i[0] += NCH
        else:
            for nch in range(NCH):
                i = psum_i[0]
                psum_i[0] += 1
                ps = ps_mm[i % 2]

                def mmgroup(e, wv=wv, ps=ps, nch=nch, rhs_sbuf=rhs_sbuf):
                    inst = None
                    for kt in range(NKD):
                        inst = e.matmul(ps, wv[:, kt * 128:(kt + 1) * 128],
                                        rhs_sbuf[:, kt * R + nch * 512: kt * R + nch * 512 + 512],
                                        start=(kt == 0), stop=(kt == NKD - 1))
                    return inst
                tok_m = r.add("tensor", mmgroup,
                              waits=[tok_w, ev_tok.get(i - 2),
                                     ev_tok.get(("t", tile_i[0] - 1)),
                                     ev_tok.get(("t", tile_i[0] - 2))] + list(extra_mm_waits), inc=sPE)
                last_m = tok_m
                eng, fn = out_cb(nch, ps)
                tok_e = r.add(eng, fn, waits=[tok_m] + list(se_extra), inc=(sSE if eng == "scalar" else sVE))
                ev_tok[i] = tok_e
                last_e = tok_e
            tile_i[0] += 1
        last_w_user[wslot] = last_m
        return last_e, last_m

    def gate_cb(func, out_ap_fn, gt):
        def cb(nch, ps):
            return ("scalar", (lambda nch, ps: lambda e: e.activation(
                out_ap_fn(nch), ps, func, bias=bg_sb[:, gt:gt + 1]))(nch, ps))
        return cb

    tok_scan = {}
    tok_u = {}
    tok_spill = {}
    tok_oload = {}
    prev_p2_last = [None]
    tok_cin = None
    tok_h = {}
    tok_cn = {}
    tok_rmsln_box = [None]
    prev_psb_reader_box = [None]
    tok_own_scan = {}

    def emit_q1_head(h):
        toks_sq = {}
        for j in range(4):
            ct_idx = h * 4 + j
            tok_sq = r.add("scalar", (lambda ct_idx: lambda e: e.activation(
                pool[4][:], A[:, ct_idx * R:(ct_idx + 1) * R], AF.Square))(ct_idx),
                waits=[tok_own_scan[ct_idx], toks_sq.get(j - 1)], inc=sSE)

            def ssq(e, j=j):
                inst = None
                for nch in range(NCH):
                    inst = e.matmul(PS_B[0:1, nch * 512:(nch + 1) * 512], ones_c[:, :],
                                    pool[4][:, nch * 512:(nch + 1) * 512],
                                    start=(j == 0), stop=(j == 3))
                return inst
            toks_sq[j] = r.add("tensor", ssq,
                               waits=[tok_sq,
                                      (prev_psb_reader_box[0] if j == 0 else None)], inc=sPE)
        tok_bc = None
        tok_inv = None
        for nch in range(NCH):
            r.add("scalar", (lambda nch: lambda e: e.activation(
                rmsrow[:], PS_B[0:1, nch * 512:(nch + 1) * 512], AF.Ln,
                scale=cst[0:1, C_INV512:C_INV512 + 1],
                bias=cst[0:1, C_EPS_RMS:C_EPS_RMS + 1]))(nch),
                waits=[toks_sq[3], tok_bc])
            r.add("scalar", lambda e: e.drain())
            tok_inv = r.add("scalar", lambda e: e.activation(rmsrow[:], rmsrow[:], AF.Exp,
                                                             scale=cst[0:1, C_NEG_HALF:C_NEG_HALF + 1]), inc=sSE)
            tok_bc = r.add("tensor", (lambda nch: lambda e: e.matmul(
                PS_B[:, nch * 512:(nch + 1) * 512], ones_r[:, :], rmsrow[0:1, :],
                start=True, stop=True))(nch),
                waits=[tok_inv], inc=sPE)
        tok_rmsln_box[0] = tok_inv
        for j in range(4):
            ct_idx = h * 4 + j
            tok_lo = r.add("sync", (lambda cbv, ct_idx: lambda e: e.dma_start(
                out=cbv[:], in_=sp_o[ct_idx, :, :]))(cbuf[j % 2], ct_idx),
                waits=[tok_spill[("o", ct_idx + 2 if j < 2 else ct_idx)],
                       tok_h.get(ct_idx - 2)], inc=s_lo[ct_idx % 2], amt=16)
            tok_oload[ct_idx] = tok_lo
            tok_cn[ct_idx] = r.add("vector", (lambda j, ct_idx: lambda e: e.scalar_tensor_tensor(
                pool[4][:], A[:, ct_idx * R:(ct_idx + 1) * R], rms_sb[:, j:j + 1], PS_B[:, :],
                ALU.mult, ALU.mult))(j, ct_idx),
                waits=[tok_bc], inc=sVE)
            tok_th = r.add("scalar", lambda e: e.activation(pool[5][:], pool[4][:], AF.Tanh),
                           waits=[tok_cn[ct_idx]], inc=sSE)
            tok_h[ct_idx] = r.add("vector", (lambda ct_idx, j: lambda e: e.tensor_mul(
                A[:, ct_idx * R:(ct_idx + 1) * R], pool[5][:], cbuf[j % 2][:]))(ct_idx, j),
                waits=[tok_th, tok_lo], inc=sVE)
        prev_psb_reader_box[0] = tok_cn[h * 4 + 3]
    tok_own_scan = {}
    own_gates_last = [None, None]

    def emit_p1(s_blk, dts=tuple(range(NKD))):
        own = (s_blk == 7)
        tok_tr = None
        for dt in dts:
            if own:
                src = xlno_hbm[dt, :, :]
            else:
                src = xlnf_hbm[dt, s_blk * R:(s_blk + 1) * R, :]
            tok_tr = r.add("sync", (lambda src, dt: lambda e: e.dma_start(
                out=A[:, dt * R:(dt + 1) * R], in_=src, transpose=True))(src, dt),
                waits=[tok_store.get(s_blk * 16 + 14), tok_store.get(s_blk * 16 + 15),
                       (prev_p2_last[0] if dt == 0 else None),
                       (tok_statdone[s_blk][0] if dt == 14 else None),
                       (tok_statdone[s_blk][1] if dt == 14 else None),
                       (tok_scan.get((s_blk - 1, HEADS - 1, 3)) if dt == 15 else None)],
                inc=s_tr, amt=16)
        return tok_tr

    for q in range(4):
        for tt in range(4 * q, 4 * q + 4):
            emit_stats_tile(0, tt)
        emit_wave(0, q)
    for tt in range(16):
        emit_apply_tile(0, tt)
    tok_tr_next = [emit_p1(0)]
    for s_blk in range(8):
        own = (s_blk == 7)
        tok_tr = tok_tr_next[0]
        p2_last = None
        for ht in range(NKD):
            def mk_cb(ht):
                def cb(nch, ps):
                    return ("scalar", (lambda ht, nch, ps: lambda e: e.copy(
                        XB[:, ht * R + nch * 512: ht * R + nch * 512 + 512], ps))(ht, nch, ps))
                return cb
            p2_ev, p2_last = emit_gemm_tile(WI[ht, :, :, :], A.ap(), mk_cb(ht), extra_mm_waits=[tok_tr])
        prev_p2_last[0] = p2_last
        tok_xp_done = p2_ev
        for h in range(HEADS):
            for j in range(4):
                w_poolj = (tok_own_scan.get(h * 4 + j - 4) if own and h > 0
                           else tok_scan.get((s_blk, h - 1, j)))
                tok_i, _ = emit_gemm_tile(WG[h * 16 + 0 * 4 + j, :, :, :], XB.ap(),
                                          gate_cb(AF.Exp, (lambda j: lambda nch: pool[j][:, nch * 512:(nch + 1) * 512])(j),
                                                  h * 16 + 0 * 4 + j),
                                          extra_mm_waits=[tok_xp_done],
                                          se_extra=[w_poolj,
                                                    tok_scan.get((s_blk - 1, HEADS - 1, j)) if h == 0 else None,
                                                    tok_cst],
                                          fast=not own)
                tok_ct, _ = emit_gemm_tile(WG[h * 16 + 3 * 4 + j, :, :, :], XB.ap(),
                                           gate_cb(AF.Tanh, lambda nch: pool[4][:, nch * 512:(nch + 1) * 512],
                                                   h * 16 + 3 * 4 + j),
                                           se_extra=[tok_u.get((s_blk, h, j - 1)),
                                                     tok_u.get((s_blk - 1, HEADS - 1, 3)) if (h == 0 and j == 0) else None,
                                                     (tok_cn.get(h * 4 - 1) if (own and j == 0) else None)],
                                           fast=not own)
                tok_u[(s_blk, h, j)] = r.add("vector", (lambda j: lambda e: e.tensor_mul(
                    pool[j][:], pool[j][:], pool[4][:]))(j), waits=[tok_i, tok_ct], inc=sVE)
                if s_blk < 7 and h < 2:
                    emit_stats_tile(s_blk + 1, h * 8 + j * 2)
                    emit_stats_tile(s_blk + 1, h * 8 + j * 2 + 1)
                    if j % 2 == 1:
                        emit_wave(s_blk + 1, h * 2 + j // 2)
            for j in range(4):
                ct_idx = h * 4 + j
                w_f = [tok_scan.get((s_blk, h, j - 1)) if j > 0 else
                       (tok_scan.get((s_blk, h - 1, 3)) if h > 0 else tok_scan.get((s_blk - 1, HEADS - 1, 3))),
                       (tok_h.get(h * 4 - 1) if (own and j == 0) else None)]
                tok_f, _ = emit_gemm_tile(WG[h * 16 + 1 * 4 + j, :, :, :], XB.ap(),
                                          gate_cb(AF.Sigmoid, lambda nch: pool[5][:, nch * 512:(nch + 1) * 512],
                                                  h * 16 + 1 * 4 + j),
                                          se_extra=w_f, fast=not own)
                if not own:
                    tok_c = r.add("vector", (lambda j, ct_idx: lambda e: e.tensor_tensor_scan(
                        dump[:], pool[5][:], pool[j][:], state[:, ct_idx:ct_idx + 1], ALU.mult, ALU.add))(j, ct_idx),
                        waits=[tok_f, tok_u[(s_blk, h, j)], tok_init if s_blk == 0 else None], inc=sVE)
                    r.add("vector", (lambda ct_idx: lambda e: e.tensor_copy(
                        state[:, ct_idx:ct_idx + 1], dump[:, R - 1:R]))(ct_idx))
                    r.add("vector", lambda e: e.drain())
                    tok_scan[(s_blk, h, j)] = r.add("vector", (lambda s_blk, ct_idx: lambda e: e.tensor_copy(
                        cand[:, s_blk * 16 + ct_idx:s_blk * 16 + ct_idx + 1],
                        state[:, ct_idx:ct_idx + 1]))(s_blk, ct_idx), inc=sVE)
                    if s_blk < 7 and h in (1, 2):
                        emit_apply_tile(s_blk + 1, (h - 1) * 8 + j * 2)
                        emit_apply_tile(s_blk + 1, (h - 1) * 8 + j * 2 + 1)
                else:
                    tok_c = r.add("vector", (lambda j, ct_idx: lambda e: e.tensor_tensor_scan(
                        A[:, ct_idx * R:(ct_idx + 1) * R], pool[5][:], pool[j][:],
                        cin_sb[:, ct_idx:ct_idx + 1], ALU.mult, ALU.add))(j, ct_idx),
                        waits=[tok_f, tok_u[(s_blk, h, j)], tok_cin, p2_last if ct_idx == 0 else None], inc=sVE)
                    tok_own_scan[ct_idx] = tok_c
                    tok_scan[(s_blk, h, j)] = tok_c
                    cbv = cbuf[j % 2]
                    tok_o, _ = emit_gemm_tile(WG[h * 16 + 2 * 4 + j, :, :, :], XB.ap(),
                                              gate_cb(AF.Sigmoid,
                                                      (lambda cbv: lambda nch: cbv[:, nch * 512:(nch + 1) * 512])(cbv),
                                                      h * 16 + 2 * 4 + j),
                                              se_extra=[tok_spill.get(("o", ct_idx - 2)),
                                                        tok_h.get(ct_idx - 2),
                                                        tok_apl[0] if ct_idx < 2 else None],
                                              fast=False)
                    tok_spill[("o", ct_idx)] = r.add("scalar", (lambda cbv, ct_idx: lambda e: e.dma_start(
                        out=sp_o[ct_idx, :, :], in_=cbv[:]))(cbv, ct_idx), waits=[tok_o],
                        inc=s_spo[ct_idx % 2], amt=16)
            if own:
                emit_q1_head(h)
            if s_blk < 7 and h == 2:
                emit_p1(s_blk + 1, dts=tuple(range(15)))
        if s_blk < 7:
            tok_tr_next[0] = emit_p1(s_blk + 1, dts=(15,))
        if s_blk == 6:
            r.add("vector", lambda e: e.nop(), waits=[tok_scan[(6, HEADS - 1, 3)], tok_cst])
            for sj in range(7):
                r.add("vector", (lambda sj: lambda e: e.tensor_scalar(
                    ta_sb[:], cand[:, sj * 16:(sj + 1) * 16], m_sb[:, sj:sj + 1], None, ALU.mult))(sj))
                r.add("vector", lambda e: e.tensor_add(cin_sb[:], cin_sb[:], ta_sb[:]))
            tok_cin = r.add("vector", lambda e: e.nop(), inc=sVE)
    own_gates_last = [last_w_user.get(0), last_w_user.get(1)]

    tok_wo = None
    for kt in range(NKD):
        tok_wo = r.add("gpsimd", (lambda kt: lambda e: e.dma_start(
            out=XB[:, kt * R:kt * R + D], in_=WO[kt, :, :]))(kt),
            waits=[own_gates_last[0] if kt == 0 else None,
                   own_gates_last[1] if kt == 0 else None], inc=s_wo, amt=16)
    tok_att_ev = {}
    tok_x2w = {}
    tok_x2lnw = {}
    tok_x2 = {}
    tok_att = None
    for tt in range(NT):
        ps = PS_A if tt % 2 == 0 else PS_B

        def attmm(e, tt=tt, ps=ps):
            inst = None
            for kt in range(NKD):
                for dch in range(NCH):
                    inst = e.matmul(ps[:, dch * 512:(dch + 1) * 512],
                                    A[:, kt * R + tt * 128: kt * R + (tt + 1) * 128],
                                    XB[:, kt * R + dch * 512: kt * R + dch * 512 + 512],
                                    start=(kt == 0), stop=(kt == NKD - 1))
            return inst
        first_guard = None
        second_guard = None
        if tt == 0:
            first_guard = ev_tok.get(psum_i[0] - 1)
            second_guard = ev_tok.get(psum_i[0] - 2)
        elif tt == 1:
            first_guard = tok_cn[15]
        tok_att = r.add("tensor", attmm,
                        waits=[tok_h[15], tok_wo, tok_att_ev.get(tt - 2), first_guard, second_guard], inc=sPE)
        att = pool[tt % 2]
        xb = pool[2 + tt % 2]
        x2 = pool[4 + tt % 2]
        tok_att_ev[tt] = r.add("vector", (lambda att, ps: lambda e: e.tensor_copy(att[:], ps[:, :]))(att, ps),
                               waits=[tok_att], inc=sVE)
        tok_x = r.add("sync", (lambda xb, tt: lambda e: e.dma_start(
            out=xb[:], in_=x_in[tt * 128:(tt + 1) * 128, :]))(xb, tt),
            waits=[tok_x2.get(tt - 2),
                   tok_own_scan.get(14) if tt == 0 else None,
                   tok_own_scan.get(15) if tt == 1 else None], inc=s_x[tt % 2], amt=16)
        r.add("vector", (lambda att: lambda e: e.reduce_sum(cols[:, 0:1], att[:], axis=AX.X))(att))
        r.add("vector", (lambda att, x2: lambda e: e.tensor_mul(x2[:], att[:], att[:]))(att, x2),
              waits=[tok_x2w.get(tt - 2)])
        r.add("vector", (lambda x2: lambda e: e.reduce_sum(cols[:, 1:2], x2[:], axis=AX.X))(x2))
        r.add("vector", lambda e: e.tensor_scalar(cols[:, 2:3], cols[:, 0:1], cst[:, C_INV2048:C_INV2048 + 1], None, ALU.mult))
        r.add("vector", lambda e: e.drain())
        r.add("vector", lambda e: e.tensor_mul(cols[:, 3:4], cols[:, 2:3], cols[:, 2:3]))
        r.add("vector", lambda e: e.drain())
        tok_v = r.add("vector", lambda e: e.tensor_scalar(cols[:, 4:5], cols[:, 1:2], cst[:, C_INV2048:C_INV2048 + 1],
                                                          cols[:, 3:4], ALU.mult, ALU.subtract), inc=sVE)
        r.add("scalar", lambda e: e.activation(cols[:, 5:6], cols[:, 4:5], AF.Ln, bias=cst[:, C_EPS_LN:C_EPS_LN + 1]),
              waits=[tok_v])
        r.add("scalar", lambda e: e.drain())
        tok_rstd = r.add("scalar", lambda e: e.activation(cols[:, 5:6], cols[:, 5:6], AF.Exp,
                                                          scale=cst[:, C_NEG_HALF:C_NEG_HALF + 1]), inc=sSE)
        r.add("vector", (lambda att: lambda e: e.tensor_scalar(att[:], att[:], cols[:, 2:3], cols[:, 5:6],
                                                               ALU.subtract, ALU.mult))(att), waits=[tok_rstd])
        tok_x2[tt] = r.add("vector", (lambda x2, att, xb: lambda e: e.tensor_add(x2[:], att[:], xb[:]))(x2, att, xb),
                           waits=[tok_x], inc=sVE)
        r.add("vector", (lambda x2: lambda e: e.reduce_sum(cols[:, 8:9], x2[:], axis=AX.X))(x2))
        r.add("vector", (lambda att, x2: lambda e: e.tensor_mul(att[:], x2[:], x2[:]))(att, x2))
        r.add("vector", (lambda att: lambda e: e.reduce_sum(cols[:, 9:10], att[:], axis=AX.X))(att))
        r.add("vector", lambda e: e.tensor_scalar(cols[:, 10:11], cols[:, 8:9], cst[:, C_INV2048:C_INV2048 + 1], None, ALU.mult))
        r.add("vector", lambda e: e.drain())
        r.add("vector", lambda e: e.tensor_mul(cols[:, 11:12], cols[:, 10:11], cols[:, 10:11]))
        r.add("vector", lambda e: e.drain())
        tok_v2 = r.add("vector", lambda e: e.tensor_scalar(cols[:, 12:13], cols[:, 9:10], cst[:, C_INV2048:C_INV2048 + 1],
                                                           cols[:, 11:12], ALU.mult, ALU.subtract), inc=sVE)
        r.add("scalar", lambda e: e.activation(cols[:, 13:14], cols[:, 12:13], AF.Ln, bias=cst[:, C_EPS_LN:C_EPS_LN + 1]),
              waits=[tok_v2])
        r.add("scalar", lambda e: e.drain())
        tok_rstd2 = r.add("scalar", lambda e: e.activation(cols[:, 13:14], cols[:, 13:14], AF.Exp,
                                                           scale=cst[:, C_NEG_HALF:C_NEG_HALF + 1]), inc=sSE)
        tok_x2ln = r.add("vector", (lambda tt, x2: lambda e: e.tensor_scalar(
            obuf[tt % 2][:], x2[:], cols[:, 10:11], cols[:, 13:14], ALU.subtract, ALU.mult))(tt, x2),
            waits=[tok_rstd2, tok_x2lnw.get(tt - 1), tok_h.get(15) if tt == 0 else None], inc=sVE)
        tok_x2w[tt] = r.add("scalar", (lambda x2, tt: lambda e: e.dma_start(
            out=x2_hbm[tt * 128:(tt + 1) * 128, :], in_=x2[:]))(x2, tt), waits=[tok_x2[tt]], inc=s_x2w[tt % 2], amt=16)
        last = None
        for dt in range(NKD):
            last = r.add("scalar", (lambda dt, tt: lambda e: e.dma_start(
                out=x2ln_hbm[dt, tt * 128:(tt + 1) * 128, :], in_=obuf[tt % 2][:, dt * 128:(dt + 1) * 128]))(dt, tt),
                waits=[tok_x2ln], inc=s_x2lnw, amt=16)
        tok_x2lnw[tt] = last

    tok_tr2 = None
    for dt in range(NKD):
        tok_tr2 = r.add("sync", (lambda dt: lambda e: e.dma_start(
            out=A[:, dt * R:(dt + 1) * R], in_=x2ln_hbm[dt, :, :], transpose=True))(dt),
            waits=[tok_x2lnw[NT - 1], tok_att if dt == 0 else None], inc=s_tr2, amt=16)

    wslot = [XB.ap()[:, i * D:(i + 1) * D] for i in range(4)]
    gslot = [XB.ap()[:, (4 + i) * D:(5 + i) * D] for i in range(2)]
    tok_pair = {}
    tok_gst = {}
    last_pair = None
    last_silu = None
    for ft in range(NF):
        wg_v = wslot[(ft % 2) * 2 + 0]
        wu_v = wslot[(ft % 2) * 2 + 1]
        tok_wg = r.add("gpsimd", (lambda ft, wg_v: lambda e: e.dma_start(
            out=wg_v.rearrange("p (kt m) -> p kt m", m=128), in_=WFG[ft, :, :, :]))(ft, wg_v),
            waits=[tok_pair.get((ft - 2, 3)), tok_att if ft == 0 else None], inc=s_wg[ft % 2], amt=16)
        tok_wu = r.add("gpsimd", (lambda ft, wu_v: lambda e: e.dma_start(
            out=wu_v.rearrange("p (kt m) -> p kt m", m=128), in_=WFU[ft, :, :, :]))(ft, wu_v),
            inc=s_wu[ft % 2], amt=16)
        gs = gslot[ft % 2]
        for nch in range(NCH):
            slot = (nch % 2) * 2
            psg = PS_A[:, slot * 512:(slot + 1) * 512]
            psu = PS_A[:, (slot + 1) * 512:(slot + 2) * 512]

            def gmm(e, wv=wg_v, psx=psg, nch=nch):
                inst = None
                for kt in range(NKD):
                    inst = e.matmul(psx, wv[:, kt * 128:(kt + 1) * 128],
                                    A[:, kt * R + nch * 512: kt * R + nch * 512 + 512],
                                    start=(kt == 0), stop=(kt == NKD - 1))
                return inst

            def umm(e, wv=wu_v, psx=psu, nch=nch):
                inst = None
                for kt in range(NKD):
                    inst = e.matmul(psx, wv[:, kt * 128:(kt + 1) * 128],
                                    A[:, kt * R + nch * 512: kt * R + nch * 512 + 512],
                                    start=(kt == 0), stop=(kt == NKD - 1))
                return inst
            prev = tok_pair.get((ft - 1, nch + 2) if nch < 2 else (ft, nch - 2))
            guard0 = None
            guard1 = tok_att_ev.get(NT - 2) if (ft == 0 and nch == 0) else None
            tok_g = r.add("tensor", gmm, waits=[tok_wg, tok_tr2, prev, guard0, guard1], inc=sPE)
            tok_um = r.add("tensor", umm, waits=[tok_wu], inc=sPE)
            tok_sg = r.add("scalar", (lambda psg: lambda e: e.activation(pool[0][:, 0:512], psg, AF.Silu))(psg),
                           waits=[tok_g, last_pair], inc=sSE)
            last_silu = tok_sg
            tok_pair[(ft, nch)] = r.add("vector", (lambda gs, nch, psu: lambda e: e.tensor_tensor(
                gs[:, nch * 512:(nch + 1) * 512], pool[0][:, 0:512], psu, ALU.mult))(gs, nch, psu),
                waits=[tok_sg, tok_um, tok_gst.get(ft - 2)], inc=sVE)
            last_pair = tok_pair[(ft, nch)]
        last = None
        for tt in range(NT):
            last = r.add("scalar", (lambda tt, ft, gs: lambda e: e.dma_start(
                out=gated_hbm[tt, :, ft, :], in_=gs[:, tt * 128:(tt + 1) * 128]))(tt, ft, gs),
                waits=[tok_pair[(ft, 3)]], inc=s_gst[ft % 2], amt=16)
        tok_gst[ft] = last

    gt_sl = [A.ap()[:, 0:NF * 128], A.ap()[:, NF * 128:2 * NF * 128]]
    wd_sl = [XB.ap(), XB.ap()]
    tok_dn_ev = {}
    tok_ffw = {}
    tok_wd_b = {}
    last_gt_user = {}
    for dq in range(4):
        tok_wd = None
        wdv = wd_sl[dq % 2]
        for kt in range(NF):
            tok_wd = r.add("gpsimd", (lambda dq, kt, wdv: lambda e: e.dma_start(
                out=wdv[:, kt * 512:(kt + 1) * 512], in_=WD[kt, :, dq * 512:(dq + 1) * 512]))(dq, kt, wdv),
                waits=[last_pair if (dq == 0 and kt == 0) else None,
                       tok_gst.get(NF - 1) if (dq == 0 and kt == 0) else None,
                       tok_gst.get(NF - 2) if (dq == 0 and kt == 0) else None,
                       tok_dn_ev.get((dq - 1, NT - 1)) if kt == 0 else None,
                       tok_dn_ev.get((dq - 1, NT - 2)) if kt == 0 else None], inc=s_wd[dq % 2], amt=16)
        tok_wd_b[dq] = tok_wd
        for tt in range(NT):
            gv = gt_sl[tt % 2]
            tok_gtl = r.add("sync", (lambda tt, gv: lambda e: e.dma_start(
                out=gv.rearrange("p (f c) -> p f c", c=128), in_=gated_hbm[tt, :, :, :]))(tt, gv),
                waits=[tok_gst.get(NF - 1) if (dq == 0 and tt == 0) else None,
                       tok_gst.get(NF - 2) if (dq == 0 and tt == 0) else None,
                       last_pair if (dq == 0 and tt < 2) else None,
                       last_gt_user.get(tt % 2)], inc=s_gt[tt % 2], amt=16)
            ps = PS_B[:, (tt % 2) * 512:(tt % 2 + 1) * 512]

            def dnmm(e, gv=gv, psx=ps, wdv=wdv):
                inst = None
                for kt in range(NF):
                    inst = e.matmul(psx, gv[:, kt * 128:(kt + 1) * 128],
                                    wdv[:, kt * 512:(kt + 1) * 512],
                                    start=(kt == 0), stop=(kt == NF - 1))
                return inst
            tok_dn = r.add("tensor", dnmm,
                           waits=[tok_gtl, tok_wd, tok_dn_ev.get((dq, tt - 2)),
                                  tok_dn_ev.get((dq - 1, NT - 2 + (tt % 2))) if tt < 2 else None],
                           inc=sPE)
            last_gt_user[tt % 2] = tok_dn
            ev = pool[tt % 2]
            tok_ev = r.add("vector", (lambda ev, psx: lambda e: e.tensor_copy(ev[:, 0:512], psx))(ev, ps),
                           waits=[tok_dn, tok_ffw.get((dq, tt - 2)),
                                  tok_ffw.get((dq - 1, NT - 2 + (tt % 2))) if tt < 2 else None,
                                  last_silu if (dq == 0 and tt < 2) else None],
                           inc=sVE)
            tok_dn_ev[(dq, tt)] = tok_ev
            tok_ffw[(dq, tt)] = r.add("scalar", (lambda ev, tt, dq: lambda e: e.dma_start(
                out=ffnout_hbm[tt * 128:(tt + 1) * 128, dq * 512:(dq + 1) * 512], in_=ev[:, 0:512]))(ev, tt, dq),
                waits=[tok_ev], inc=s_ffw[tt % 2], amt=16)

    tok_outw = {}
    tok_ot = {}
    for tt in range(NT):
        ffn = pool[tt % 2]
        x2 = pool[2 + tt % 2]
        ot = pool[4 + tt % 2]
        ffw_done = ([tok_ffw[(dq, tp)] for dq in range(4) for tp in (NT - 2, NT - 1)]
                    if tt == 0 else [])
        tok_lf = r.add("sync", (lambda ffn, tt: lambda e: e.dma_start(
            out=ffn[:], in_=ffnout_hbm[tt * 128:(tt + 1) * 128, :]))(ffn, tt),
            waits=ffw_done + [tok_outw.get(tt - 2),
                              tok_dn_ev.get((3, NT - 2 + (tt % 2))) if tt < 2 else None],
            inc=s_lf[tt % 2], amt=16)
        tok_lx2 = r.add("sync", (lambda x2, tt: lambda e: e.dma_start(
            out=x2[:], in_=x2_hbm[tt * 128:(tt + 1) * 128, :]))(x2, tt), inc=s_lx2[tt % 2], amt=16)
        r.add("vector", (lambda ffn: lambda e: e.reduce_sum(cols[:, 0:1], ffn[:], axis=AX.X))(ffn),
              waits=[tok_lf])
        tok_q6sq = r.add("scalar", (lambda ffn, tt: lambda e: e.activation(
            A[:, 2 * NF * 128 + (tt % 2) * R:2 * NF * 128 + (tt % 2 + 1) * R],
            ffn[:], AF.Square, accum_out=cols[:, 1:2]))(ffn, tt),
            waits=[tok_lf, tok_outw.get(tt - 2)], inc=sSE)
        r.add("vector", lambda e: e.nop(), waits=[tok_q6sq])
        r.add("vector", lambda e: e.tensor_scalar(cols[:, 2:3], cols[:, 0:1], cst[:, C_INV2048:C_INV2048 + 1], None, ALU.mult))
        r.add("vector", lambda e: e.drain())
        r.add("vector", lambda e: e.tensor_mul(cols[:, 3:4], cols[:, 2:3], cols[:, 2:3]))
        r.add("vector", lambda e: e.drain())
        tok_v = r.add("vector", lambda e: e.tensor_scalar(cols[:, 4:5], cols[:, 1:2], cst[:, C_INV2048:C_INV2048 + 1],
                                                          cols[:, 3:4], ALU.mult, ALU.subtract), inc=sVE)
        r.add("scalar", lambda e: e.activation(cols[:, 5:6], cols[:, 4:5], AF.Ln, bias=cst[:, C_EPS_LN:C_EPS_LN + 1]),
              waits=[tok_v])
        r.add("scalar", lambda e: e.drain())
        tok_rstd = r.add("scalar", lambda e: e.activation(cols[:, 5:6], cols[:, 5:6], AF.Exp,
                                                          scale=cst[:, C_NEG_HALF:C_NEG_HALF + 1]), inc=sSE)
        r.add("vector", (lambda ffn: lambda e: e.tensor_scalar(ffn[:], ffn[:], cols[:, 2:3], cols[:, 5:6],
                                                               ALU.subtract, ALU.mult))(ffn), waits=[tok_rstd])
        tok_ot[tt] = r.add("vector", (lambda ot, ffn, x2: lambda e: e.tensor_add(ot[:], ffn[:], x2[:]))(ot, ffn, x2),
                           waits=[tok_lx2], inc=sVE)
        tok_outw[tt] = r.add("sync", (lambda ot, tt: lambda e: e.dma_start(
            out=out[tt * 128:(tt + 1) * 128, :], in_=ot[:]))(ot, tt), waits=[tok_ot[tt]], inc=s_outw[tt % 2], amt=16)

    r.add("sync", lambda e: e.nop(), waits=r.final_dma_waits())

    with nc.Block() as block:
        r.emit(nc, block)
    return nc



def kernel(L0_attn_x, L0_attn_W_in, L0_attn_W_gate, L0_attn_b_gate, L0_attn_rms_w,
           L0_attn_W_out, L0_ffn_w_gate, L0_ffn_w_up, L0_ffn_w_down):
    x = np.ascontiguousarray(np.asarray(L0_attn_x, np.float32).reshape(S, D))
    WI_t = tile_lhsT(np.asarray(L0_attn_W_in, np.float32))
    WG_t = tile_lhsT(np.asarray(L0_attn_W_gate, np.float32))
    bgT = np.ascontiguousarray(np.asarray(L0_attn_b_gate, np.float32).reshape(NG, 128).T)
    rmsT = np.ascontiguousarray(np.asarray(L0_attn_rms_w, np.float32).reshape(4, 128).T)
    WO_p = np.ascontiguousarray(np.asarray(L0_attn_W_out, np.float32).T.reshape(NKD, 128, D))
    WFG_t = tile_lhsT(np.asarray(L0_ffn_w_gate, np.float32))
    WFU_t = tile_lhsT(np.asarray(L0_ffn_w_up, np.float32))
    WD_p = np.ascontiguousarray(np.asarray(L0_ffn_w_down, np.float32).T.reshape(NF, 128, D))
    consts = np.ascontiguousarray(np.tile(np.array(
        [1e-6, 1e-5, -0.5, 1.0 / 512, 1.0 / 2048, 0.0, 1.0, 0.0], np.float32), (128, 1)))

    nc = build_fused()
    in_maps = []
    for k in range(NC):
        msel = np.zeros((128, 8), np.float32)
        if k > 0:
            msel[:, k - 1] = 1.0
        in_maps.append({
            "xf": x, "x": x[k * R:(k + 1) * R],
            "WI": WI_t, "WG": WG_t, "bgT": bgT, "mselt": msel,
            "rmsT": rmsT, "WO": WO_p, "WFG": WFG_t, "WFU": WFU_t, "WD": WD_p,
            "consts": consts,
        })
    res = run_bass_kernel_spmd(nc, in_maps, core_ids=list(range(NC)))
    out = np.concatenate([np.asarray(res.results[k]["out"]) for k in range(NC)], axis=0)
    return out.reshape(1, S, D)


# version 17 (final)
# speedup vs baseline: 1.1790x; 1.1790x over previous
import contextlib
import numpy as np
import concourse.bass as bass
import concourse.mybir as mybir
from concourse.bass_utils import run_bass_kernel_spmd

F32 = mybir.dt.float32
BF16 = mybir.dt.bfloat16
AF = mybir.ActivationFunctionType
ALU = mybir.AluOpType
AX = mybir.AxisListType

NC = 8
S = 16384
D = 2048
R = S // NC
G = 8192
HF = 8192
NT = R // 128
NKD = D // 128
NG = G // 128
NF = HF // 128
NCH = 4
HEADS = 4

C_EPS_RMS, C_EPS_LN, C_NEG_HALF, C_INV512, C_INV2048 = 0, 1, 2, 3, 4


class Sem:
    __slots__ = ("name", "n", "h", "owner")

    def __init__(self, name, owner=None):
        self.name, self.n, self.h, self.owner = name, 0, None, owner


class Rec:
    ENGS = ("sync", "gpsimd", "vector", "scalar", "tensor")

    def __init__(self):
        self.ops = {e: [] for e in self.ENGS}
        self.sems = []

    def sem(self, name, owner=None):
        s = Sem(name, owner)
        self.sems.append(s)
        return s

    def dsems(self, name, k):
        return [self.sem(f"{name}{i}") for i in range(k)]

    def add(self, eng, fn, waits=(), inc=None, amt=1):
        tok = None
        if inc is not None:
            inc.n += amt
            tok = (inc, inc.n)
        self.ops[eng].append((fn, [w for w in waits if w is not None], inc, amt))
        return tok

    def final_dma_waits(self):
        return [(s, s.n) for s in self.sems if s.owner is None and s.n > 0]

    def emit(self, nc, block):
        ctx = contextlib.ExitStack()
        for s in self.sems:
            s.h = ctx.enter_context(nc.semaphore(s.name))

        def make_body(ename):
            def body(eng):
                waited = {}
                for fn, ws, inc, amt in self.ops[ename]:
                    for (s, v) in ws:
                        if s.owner == ename:
                            continue
                        if waited.get(id(s), -1) >= v:
                            continue
                        eng.wait_ge(s.h, v)
                        waited[id(s)] = v
                    inst = fn(eng)
                    if inc is not None:
                        if inc.owner is None:
                            if inst is None:
                                inst = eng.nop()
                            inst.then_inc(inc.h, amt)
                        else:
                            eng.drain().then_inc(inc.h, amt)
                    elif ename in ("vector", "scalar", "gpsimd"):
                        eng.drain()
            return body

        for ename in self.ENGS:
            if self.ops[ename]:
                getattr(block, ename)(make_body(ename))
        return ctx


def tile_lhsT(W):
    OUT, K = W.shape
    arr = W.reshape(OUT // 128, 128, K // 128, 128)
    return np.ascontiguousarray(arr.transpose(0, 3, 2, 1))


def build_launch_A():
    nc = bass.Bass("TRN2", target_bir_lowering=False)
    x_in = nc.dram_tensor("x", [R, D], F32, kind="ExternalInput").ap()
    WI = nc.dram_tensor("WI", [NKD, 128, NKD, 128], F32, kind="ExternalInput").ap()
    WG = nc.dram_tensor("WG", [NG, 128, NKD, 128], F32, kind="ExternalInput").ap()
    bgT = nc.dram_tensor("bgT", [128, NG], F32, kind="ExternalInput").ap()
    consts = nc.dram_tensor("consts", [128, 8], F32, kind="ExternalInput").ap()

    sp_c = nc.dram_tensor("sp_c", [16, 128, R], BF16, kind="ExternalOutput").ap()
    sp_P = nc.dram_tensor("sp_P", [16, 128, R], BF16, kind="ExternalOutput").ap()
    sp_o = nc.dram_tensor("sp_o", [16, 128, R], BF16, kind="ExternalOutput").ap()
    exch = nc.dram_tensor("exch", [128, 32], F32, kind="ExternalOutput").ap()

    xln_hbm = nc.dram_tensor("xln_hbm", [NKD, R, 128], BF16).ap()

    r = Rec()
    sVE = r.sem("sVE", "vector")
    sSE = r.sem("sSE", "scalar")
    sPE = r.sem("sPE", "tensor")
    s_cst = r.sem("s_cst")
    s_x = r.dsems("s_x", 2)
    s_st = r.dsems("s_st", 2)
    s_tr = r.sem("s_tr")
    s_w = r.dsems("s_w", 2)
    s_spc = r.dsems("s_spc", 2)
    s_spP = r.dsems("s_spP", 2)
    s_spo = r.dsems("s_spo", 2)
    s_ex = r.sem("s_ex")

    st = contextlib.ExitStack()
    A = st.enter_context(nc.sbuf_tensor("A", [128, NKD * R], BF16))
    XP = st.enter_context(nc.sbuf_tensor("XP", [128, NKD * R], BF16))
    pool = [st.enter_context(nc.sbuf_tensor(f"T{i}", [128, R], F32)) for i in range(6)]
    cbuf = [st.enter_context(nc.sbuf_tensor(f"cb{i}", [128, R], BF16)) for i in range(2)]
    pbuf = [st.enter_context(nc.sbuf_tensor(f"pb{i}", [128, R], BF16)) for i in range(2)]
    wbuf = [st.enter_context(nc.sbuf_tensor(f"wb{i}", [128, NKD * 128], BF16)) for i in range(2)]
    cols = st.enter_context(nc.sbuf_tensor("cols", [128, 16], F32))
    cst = st.enter_context(nc.sbuf_tensor("cst", [128, 8], F32))
    bg_sb = st.enter_context(nc.sbuf_tensor("bg_sb", [128, NG], F32))
    exch_sb = st.enter_context(nc.sbuf_tensor("exch_sb", [128, 32], F32))
    wrm = st.enter_context(nc.sbuf_tensor("wrm", [128, 8], F32))
    ps_mm = [st.enter_context(nc.psum_tensor(f"ps{i}", [128, 512], F32)) for i in range(2)]

    r.add("sync", lambda e: e.dma_start(out=cst[:], in_=consts[:, :]), inc=s_cst, amt=16)
    tok_cst = r.add("sync", lambda e: e.dma_start(out=bg_sb[:], in_=bgT[:, :]), inc=s_cst, amt=16)
    r.add("scalar", lambda e: e.copy(wrm[:, 0:1], cst[:, 0:1]), waits=[tok_cst])
    r.add("scalar", lambda e: e.copy(wrm[:, 1:2], cst[:, 0:1]))
    r.add("vector", lambda e: e.memset(wrm[:, 2:3], 0.0))
    r.add("gpsimd", lambda e: e.memset(wrm[:, 4:5], 0.0))

    tok_apply = {}
    tok_store = {}
    for tt in range(NT):
        sl = tt % 2
        xb = pool[sl]
        stg = sl * R
        tok_x = r.add("sync", (lambda xb, tt: lambda e: e.dma_start(
            out=xb[:], in_=x_in[tt * 128:(tt + 1) * 128, :]))(xb, tt),
            waits=[tok_apply.get(tt - 2)], inc=s_x[sl], amt=16)
        scr = pool[2]
        r.add("vector", (lambda xb: lambda e: e.reduce_sum(cols[:, 0:1], xb[:], axis=AX.X))(xb),
              waits=[tok_x, tok_cst])
        r.add("vector", (lambda xb, scr: lambda e: e.tensor_mul(scr[:], xb[:], xb[:]))(xb, scr))
        r.add("vector", (lambda scr: lambda e: e.reduce_sum(cols[:, 1:2], scr[:], axis=AX.X))(scr))
        r.add("vector", lambda e: e.tensor_scalar(cols[:, 2:3], cols[:, 0:1], cst[:, C_INV2048:C_INV2048 + 1], None, ALU.mult))
        r.add("vector", lambda e: e.drain())
        r.add("vector", lambda e: e.tensor_mul(cols[:, 3:4], cols[:, 2:3], cols[:, 2:3]))
        r.add("vector", lambda e: e.drain())
        tok_v = r.add("vector", lambda e: e.tensor_scalar(cols[:, 4:5], cols[:, 1:2], cst[:, C_INV2048:C_INV2048 + 1],
                                                          cols[:, 3:4], ALU.mult, ALU.subtract), inc=sVE)
        r.add("scalar", lambda e: e.activation(cols[:, 5:6], cols[:, 4:5], AF.Ln, bias=cst[:, C_EPS_LN:C_EPS_LN + 1]),
              waits=[tok_v])
        r.add("scalar", lambda e: e.drain())
        tok_rstd = r.add("scalar", lambda e: e.activation(cols[:, 5:6], cols[:, 5:6], AF.Exp,
                                                          scale=cst[:, C_NEG_HALF:C_NEG_HALF + 1]), inc=sSE)
        tok_apply[tt] = r.add("vector", (lambda xb, stg: lambda e: e.tensor_scalar(
            A[:, stg:stg + D], xb[:], cols[:, 2:3], cols[:, 5:6], ALU.subtract, ALU.mult))(xb, stg),
            waits=[tok_rstd, tok_store.get(tt - 2)], inc=sVE)
        tok_w = None
        for dt in range(NKD):
            tok_w = r.add("sync", (lambda dt, tt, stg: lambda e: e.dma_start(
                out=xln_hbm[dt, tt * 128:(tt + 1) * 128, :],
                in_=A[:, stg + dt * 128:stg + (dt + 1) * 128]))(dt, tt, stg),
                waits=[tok_apply[tt]], inc=s_st[sl], amt=16)
        tok_store[tt] = tok_w

    tok_tr = None
    for dt in range(NKD):
        tok_tr = r.add("sync", (lambda dt: lambda e: e.dma_start(
            out=A[:, dt * R:(dt + 1) * R], in_=xln_hbm[dt, :, :], transpose=True))(dt),
            waits=[tok_store[NT - 2], tok_store[NT - 1]], inc=s_tr, amt=16)

    ev_tok = {}
    psum_i = [0]
    last_w_user = {}

    def emit_gemm_tile(weight_dram_block, rhs_sbuf, out_cb, extra_mm_waits=(), se_extra=()):
        wslot = (psum_i[0] // NCH) % 2
        wv = wbuf[wslot]
        tok_w = r.add("gpsimd", (lambda wv, weight_dram_block: lambda e: e.dma_start(
            out=wv.ap().rearrange("p (kt m) -> p kt m", m=128), in_=weight_dram_block))(wv, weight_dram_block),
            waits=[last_w_user.get(wslot)], inc=s_w[wslot], amt=16)
        last_e = None
        last_m = None
        for nch in range(NCH):
            i = psum_i[0]
            psum_i[0] += 1
            ps = ps_mm[i % 2]

            def mmgroup(e, wv=wv, ps=ps, nch=nch, rhs_sbuf=rhs_sbuf):
                inst = None
                for kt in range(NKD):
                    inst = e.matmul(ps[:, :], wv[:, kt * 128:(kt + 1) * 128],
                                    rhs_sbuf[:, kt * R + nch * 512: kt * R + nch * 512 + 512],
                                    start=(kt == 0), stop=(kt == NKD - 1))
                return inst
            tok_m = r.add("tensor", mmgroup,
                          waits=[tok_w, ev_tok.get(i - 2)] + list(extra_mm_waits), inc=sPE)
            last_m = tok_m
            eng, fn = out_cb(nch, ps)
            tok_e = r.add(eng, fn, waits=[tok_m] + list(se_extra), inc=(sSE if eng == "scalar" else sVE))
            ev_tok[i] = tok_e
            last_e = tok_e
        last_w_user[wslot] = last_m
        return last_e

    for ht in range(NKD):
        def mk_cb(ht):
            def cb(nch, ps):
                return ("scalar", (lambda ht, nch, ps: lambda e: e.copy(
                    XP[:, ht * R + nch * 512: ht * R + nch * 512 + 512], ps[:, :]))(ht, nch, ps))
            return cb
        emit_gemm_tile(WI[ht, :, :, :], A.ap(), mk_cb(ht), extra_mm_waits=[tok_tr])

    tok_xp_done = ev_tok[NKD * NCH - 1]

    def gate_cb(func, out_ap_fn, gt):
        def cb(nch, ps):
            return ("scalar", (lambda nch, ps: lambda e: e.activation(
                out_ap_fn(nch), ps[:, :], func, bias=bg_sb[:, gt:gt + 1]))(nch, ps))
        return cb

    tok_scan = {}
    tok_u = {}
    tok_spill = {}
    tok_exch = None
    for h in range(HEADS):
        for j in range(4):
            tok_i = emit_gemm_tile(WG[h * 16 + 0 * 4 + j, :, :, :], XP.ap(),
                                   gate_cb(AF.Exp, (lambda j: lambda nch: pool[j][:, nch * 512:(nch + 1) * 512])(j),
                                           h * 16 + 0 * 4 + j),
                                   extra_mm_waits=[tok_xp_done],
                                   se_extra=[tok_scan.get(("c", h - 1, j)), tok_cst])
            tok_ct = emit_gemm_tile(WG[h * 16 + 3 * 4 + j, :, :, :], XP.ap(),
                                    gate_cb(AF.Tanh, lambda nch: pool[4][:, nch * 512:(nch + 1) * 512],
                                            h * 16 + 3 * 4 + j),
                                    se_extra=[tok_u.get((h, j - 1)), tok_scan.get(("P", h - 1, 3))])
            tok_u[(h, j)] = r.add("vector", (lambda j: lambda e: e.tensor_mul(pool[j][:], pool[j][:], pool[4][:]))(j),
                                  waits=[tok_i, tok_ct], inc=sVE)
        tok_z = r.add("vector", lambda e: e.memset(pool[4][:], 0.0), waits=[tok_u[(h, 3)]], inc=sVE)
        for j in range(4):
            ct_idx = h * 4 + j
            w_f = [tok_scan.get(("P", h, j - 1)) if j > 0 else tok_scan.get(("P", h - 1, 3))]
            tok_f = emit_gemm_tile(WG[h * 16 + 1 * 4 + j, :, :, :], XP.ap(),
                                   gate_cb(AF.Sigmoid, lambda nch: pool[5][:, nch * 512:(nch + 1) * 512],
                                           h * 16 + 1 * 4 + j),
                                   se_extra=w_f)
            cb_t = cbuf[j % 2]
            pb_t = pbuf[j % 2]
            tok_c = r.add("vector", (lambda cb_t, j: lambda e: e.tensor_tensor_scan(
                cb_t[:], pool[5][:], pool[j][:], 0.0, ALU.mult, ALU.add))(cb_t, j),
                waits=[tok_f, tok_u[(h, j)], tok_spill.get(("c", ct_idx - 2))], inc=sVE)
            tok_scan[("c", h, j)] = tok_c
            r.add("vector", (lambda cb_t, ct_idx: lambda e: e.tensor_copy(
                exch_sb[:, ct_idx:ct_idx + 1], cb_t[:, R - 1:R]))(cb_t, ct_idx), inc=sVE)
            tok_P = r.add("vector", (lambda pb_t, j: lambda e: e.tensor_tensor_scan(
                pb_t[:], pool[5][:], pool[4][:], 1.0, ALU.mult, ALU.add))(pb_t, j),
                waits=[tok_z, tok_spill.get(("P", ct_idx - 2))], inc=sVE)
            tok_scan[("P", h, j)] = tok_P
            tok_exch = r.add("vector", (lambda pb_t, ct_idx: lambda e: e.tensor_copy(
                exch_sb[:, 16 + ct_idx:17 + ct_idx], pb_t[:, R - 1:R]))(pb_t, ct_idx), inc=sVE)
            tok_spill[("c", ct_idx)] = r.add("scalar", (lambda cb_t, ct_idx: lambda e: e.dma_start(
                out=sp_c[ct_idx, :, :], in_=cb_t[:]))(cb_t, ct_idx), waits=[tok_c], inc=s_spc[j % 2], amt=16)
            tok_spill[("P", ct_idx)] = r.add("scalar", (lambda pb_t, ct_idx: lambda e: e.dma_start(
                out=sp_P[ct_idx, :, :], in_=pb_t[:]))(pb_t, ct_idx), waits=[tok_P], inc=s_spP[j % 2], amt=16)
            ob_off = (ct_idx % 2) * R
            tok_o = emit_gemm_tile(WG[h * 16 + 2 * 4 + j, :, :, :], XP.ap(),
                                   gate_cb(AF.Sigmoid,
                                           (lambda ob_off: lambda nch: A[:, ob_off + nch * 512: ob_off + (nch + 1) * 512])(ob_off),
                                           h * 16 + 2 * 4 + j),
                                   se_extra=[tok_spill.get(("o", ct_idx - 2))])
            tok_spill[("o", ct_idx)] = r.add("scalar", (lambda ob_off, ct_idx: lambda e: e.dma_start(
                out=sp_o[ct_idx, :, :], in_=A[:, ob_off:ob_off + R]))(ob_off, ct_idx),
                waits=[tok_o], inc=s_spo[ct_idx % 2], amt=16)

    r.add("sync", lambda e: e.dma_start(out=exch[:, :], in_=exch_sb[:]),
          waits=[tok_exch], inc=s_ex, amt=16)
    r.add("sync", lambda e: e.nop(), waits=r.final_dma_waits())

    with nc.Block() as block:
        r.emit(nc, block)
    return nc


def build_launch_B():
    nc = bass.Bass("TRN2", target_bir_lowering=False)
    x_in = nc.dram_tensor("x", [R, D], F32, kind="ExternalInput").ap()
    sp_c = nc.dram_tensor("sp_c", [16, 128, R], BF16, kind="ExternalInput").ap()
    sp_P = nc.dram_tensor("sp_P", [16, 128, R], BF16, kind="ExternalInput").ap()
    sp_o = nc.dram_tensor("sp_o", [16, 128, R], BF16, kind="ExternalInput").ap()
    exch_all = nc.dram_tensor("exch_all", [NC * 128, 32], F32, kind="ExternalInput").ap()
    mks = nc.dram_tensor("mks", [128, 8], F32, kind="ExternalInput").ap()
    mks1 = nc.dram_tensor("mks1", [128, 8], F32, kind="ExternalInput").ap()
    rmsT = nc.dram_tensor("rmsT", [128, 4], F32, kind="ExternalInput").ap()
    WO = nc.dram_tensor("WO", [NKD, 128, D], F32, kind="ExternalInput").ap()
    WFG = nc.dram_tensor("WFG", [NF, 128, NKD, 128], F32, kind="ExternalInput").ap()
    WFU = nc.dram_tensor("WFU", [NF, 128, NKD, 128], F32, kind="ExternalInput").ap()
    WD = nc.dram_tensor("WD", [NF, 128, D], F32, kind="ExternalInput").ap()
    consts = nc.dram_tensor("consts", [128, 8], F32, kind="ExternalInput").ap()

    out = nc.dram_tensor("out", [R, D], F32, kind="ExternalOutput").ap()

    x2_hbm = nc.dram_tensor("x2_hbm", [R, D], F32).ap()
    x2ln_hbm = nc.dram_tensor("x2ln_hbm", [NKD, R, 128], BF16).ap()
    gated_hbm = nc.dram_tensor("gated_hbm", [NT, 128, NF, 128], BF16).ap()
    ffnout_hbm = nc.dram_tensor("ffnout_hbm", [R, D], F32).ap()

    r = Rec()
    sVE = r.sem("sVE", "vector")
    sSE = r.sem("sSE", "scalar")
    sPE = r.sem("sPE", "tensor")
    s_cst = r.sem("s_cst")
    s_lc = r.dsems("s_lc", 2)
    s_lp = r.dsems("s_lp", 2)
    s_lo = r.sem("s_lo")
    s_wo = r.sem("s_wo")
    s_x = r.dsems("s_x", 2)
    s_x2w = r.dsems("s_x2w", 2)
    s_x2lnw = r.sem("s_x2lnw")
    s_tr2 = r.sem("s_tr2")
    s_wg = r.dsems("s_wg", 2)
    s_wu = r.dsems("s_wu", 2)
    s_gst = r.dsems("s_gst", 2)
    s_wd = r.sem("s_wd")
    s_gt = r.dsems("s_gt", 2)
    s_ffw = r.dsems("s_ffw", 2)
    s_lf = r.dsems("s_lf", 2)
    s_lx2 = r.dsems("s_lx2", 2)
    s_outw = r.dsems("s_outw", 2)

    st = contextlib.ExitStack()
    A = st.enter_context(nc.sbuf_tensor("A", [128, NKD * R], BF16))
    BIG = st.enter_context(nc.sbuf_tensor("BIG", [128, NKD * R], BF16))
    pool = [st.enter_context(nc.sbuf_tensor(f"T{i}", [128, R], F32)) for i in range(6)]
    cbuf = [st.enter_context(nc.sbuf_tensor(f"cb{i}", [128, R], BF16)) for i in range(2)]
    pbuf = [st.enter_context(nc.sbuf_tensor(f"pb{i}", [128, R], BF16)) for i in range(2)]
    ob0 = st.enter_context(nc.sbuf_tensor("ob0", [128, R], BF16))
    obuf = [ob0, ob0]
    cols = st.enter_context(nc.sbuf_tensor("cols", [128, 16], F32))
    cst = st.enter_context(nc.sbuf_tensor("cst", [128, 8], F32))
    m_sb = st.enter_context(nc.sbuf_tensor("m_sb", [128, 8], F32))
    m1_sb = st.enter_context(nc.sbuf_tensor("m1_sb", [128, 8], F32))
    rms_sb = st.enter_context(nc.sbuf_tensor("rms_sb", [128, 4], F32))
    exa_sb = st.enter_context(nc.sbuf_tensor("exa_sb", [128, NC * 32], F32))
    cin_sb = st.enter_context(nc.sbuf_tensor("cin_sb", [128, 16], F32))
    ta_sb = st.enter_context(nc.sbuf_tensor("ta_sb", [128, 16], F32))
    tb_sb = st.enter_context(nc.sbuf_tensor("tb_sb", [128, 16], F32))
    ones_c = st.enter_context(nc.sbuf_tensor("ones_c", [128, 1], F32))
    ones_r = st.enter_context(nc.sbuf_tensor("ones_r", [1, 128], F32))
    row2 = st.enter_context(nc.sbuf_tensor("row2", [1, D], F32))
    wrm = st.enter_context(nc.sbuf_tensor("wrm", [128, 8], F32))
    PS_A = st.enter_context(nc.psum_tensor("PS_A", [128, D], F32))
    PS_B = st.enter_context(nc.psum_tensor("PS_B", [128, D], F32))

    r.add("sync", lambda e: e.dma_start(out=cst[:], in_=consts[:, :]), inc=s_cst, amt=16)
    r.add("sync", lambda e: e.dma_start(out=m_sb[:], in_=mks[:, :]), inc=s_cst, amt=16)
    r.add("sync", lambda e: e.dma_start(out=m1_sb[:], in_=mks1[:, :]), inc=s_cst, amt=16)
    r.add("sync", lambda e: e.dma_start(out=rms_sb[:], in_=rmsT[:, :]), inc=s_cst, amt=16)
    tok_cst = r.add("sync", lambda e: e.dma_start(
        out=exa_sb.ap().rearrange("p (r c) -> p r c", c=32),
        in_=exch_all.rearrange("(r p) c -> p r c", p=128)), inc=s_cst, amt=16)
    r.add("scalar", lambda e: e.copy(wrm[:, 0:1], cst[:, 0:1]), waits=[tok_cst])
    r.add("scalar", lambda e: e.copy(wrm[:, 1:2], cst[:, 0:1]))
    r.add("gpsimd", lambda e: e.memset(wrm[:, 4:5], 0.0))
    r.add("vector", lambda e: e.memset(cin_sb[:], 0.0))
    r.add("vector", lambda e: e.memset(ones_c[:], 1.0))
    tok_ones = r.add("vector", lambda e: e.memset(ones_r[:], 1.0), inc=sVE)
    r.add("vector", lambda e: e.nop(), waits=[tok_cst])
    for j in range(NC - 1):
        r.add("vector", (lambda j: lambda e: e.tensor_scalar(
            ta_sb[:], exa_sb[:, j * 32 + 16:j * 32 + 32], m_sb[:, j:j + 1], m1_sb[:, j:j + 1], ALU.mult, ALU.add))(j))
        r.add("vector", (lambda j: lambda e: e.tensor_scalar(
            tb_sb[:], exa_sb[:, j * 32:j * 32 + 16], m_sb[:, j:j + 1], None, ALU.mult))(j))
        r.add("vector", lambda e: e.drain())
        r.add("vector", lambda e: e.tensor_mul(cin_sb[:], cin_sb[:], ta_sb[:]))
        r.add("vector", lambda e: e.drain())
        r.add("vector", lambda e: e.tensor_add(cin_sb[:], cin_sb[:], tb_sb[:]))
        r.add("vector", lambda e: e.drain())
    tok_cin = r.add("vector", lambda e: e.nop(), inc=sVE)

    tok_h = {}
    tok_cn = {}
    tok_fix = {}
    tok_rmsln = None
    prev_psb_reader = None
    for h in range(HEADS):
        toks_sq = {}
        for j in range(4):
            ct_idx = h * 4 + j
            tok_lc = r.add("sync", (lambda cbt, ct_idx: lambda e: e.dma_start(
                out=cbt[:], in_=sp_c[ct_idx, :, :]))(cbuf[j % 2], ct_idx),
                waits=[tok_fix.get(ct_idx - 2)], inc=s_lc[j % 2], amt=16)
            tok_lp = r.add("sync", (lambda pbt, ct_idx: lambda e: e.dma_start(
                out=pbt[:], in_=sp_P[ct_idx, :, :]))(pbuf[j % 2], ct_idx), inc=s_lp[j % 2], amt=16)
            tok_fix[ct_idx] = r.add("vector", (lambda j, ct_idx: lambda e: e.scalar_tensor_tensor(
                pool[j][:], pbuf[j % 2][:], cin_sb[:, ct_idx:ct_idx + 1], cbuf[j % 2][:], ALU.mult, ALU.add))(j, ct_idx),
                waits=[tok_lc, tok_lp, tok_cin, tok_cn.get(ct_idx - 4)], inc=sVE)
            tok_sq = r.add("scalar", (lambda j: lambda e: e.activation(pool[4][:], pool[j][:], AF.Square))(j),
                           waits=[tok_fix[ct_idx], toks_sq.get(j - 1)], inc=sSE)

            def ssq(e, j=j):
                inst = None
                for nch in range(NCH):
                    inst = e.matmul(PS_A[0:1, nch * 512:(nch + 1) * 512], ones_c[:, :],
                                    pool[4][:, nch * 512:(nch + 1) * 512],
                                    start=(j == 0), stop=(j == 3))
                return inst
            toks_sq[j] = r.add("tensor", ssq, waits=[tok_sq, tok_rmsln if j == 0 else None,
                                                      tok_ones if ct_idx == 0 else None], inc=sPE)
        tok_bc = None
        for nch in range(NCH):
            r.add("scalar", (lambda nch: lambda e: e.activation(
                rmsrow[:], PS_A[0:1, nch * 512:(nch + 1) * 512], AF.Ln,
                scale=cst[0:1, C_INV512:C_INV512 + 1],
                bias=cst[0:1, C_EPS_RMS:C_EPS_RMS + 1]))(nch),
                waits=[toks_sq[3], tok_bc])
            r.add("scalar", lambda e: e.drain())
            tok_inv = r.add("scalar", lambda e: e.activation(rmsrow[:], rmsrow[:], AF.Exp,
                                                             scale=cst[0:1, C_NEG_HALF:C_NEG_HALF + 1]), inc=sSE)
            tok_bc = r.add("tensor", (lambda nch: lambda e: e.matmul(
                PS_B[:, nch * 512:(nch + 1) * 512], ones_r[:, :], rmsrow[0:1, :],
                start=True, stop=True))(nch),
                waits=[tok_inv, prev_psb_reader if nch == 0 else None], inc=sPE)
        tok_rmsln = tok_inv
        for j in range(4):
            ct_idx = h * 4 + j
            tok_lo = r.add("sync", (lambda obt, ct_idx: lambda e: e.dma_start(
                out=obt[:], in_=sp_o[ct_idx, :, :]))(obuf[j % 2], ct_idx),
                waits=[tok_h.get(ct_idx - 1)], inc=s_lo, amt=16)
            tok_cn[ct_idx] = r.add("vector", (lambda j: lambda e: e.scalar_tensor_tensor(
                pool[4][:], pool[j][:], rms_sb[:, j:j + 1], PS_B[:, :], ALU.mult, ALU.mult))(j),
                waits=[tok_bc], inc=sVE)
            tok_th = r.add("scalar", lambda e: e.activation(pool[5][:], pool[4][:], AF.Tanh),
                           waits=[tok_cn[ct_idx]], inc=sSE)
            tok_h[ct_idx] = r.add("vector", (lambda ct_idx, j: lambda e: e.tensor_mul(
                A[:, ct_idx * R:(ct_idx + 1) * R], pool[5][:], obuf[j % 2][:]))(ct_idx, j),
                waits=[tok_th, tok_lo], inc=sVE)
        prev_psb_reader = tok_cn[h * 4 + 3]

    tok_wo = None
    for kt in range(NKD):
        tok_wo = r.add("gpsimd", (lambda kt: lambda e: e.dma_start(
            out=BIG[:, kt * R:kt * R + D], in_=WO[kt, :, :]))(kt), inc=s_wo, amt=16)
    tok_att_ev = {}
    tok_x2w = {}
    tok_x2lnw = {}
    tok_x2 = {}
    tok_att = None
    for tt in range(NT):
        ps = PS_A if tt % 2 == 0 else PS_B

        def attmm(e, tt=tt, ps=ps):
            inst = None
            for kt in range(NKD):
                for dch in range(NCH):
                    inst = e.matmul(ps[:, dch * 512:(dch + 1) * 512],
                                    A[:, kt * R + tt * 128: kt * R + (tt + 1) * 128],
                                    BIG[:, kt * R + dch * 512: kt * R + dch * 512 + 512],
                                    start=(kt == 0), stop=(kt == NKD - 1))
            return inst
        first_guard = None
        if tt == 0:
            first_guard = tok_rmsln
        elif tt == 1:
            first_guard = tok_cn[15]
        tok_att = r.add("tensor", attmm,
                        waits=[tok_h[15], tok_wo, tok_att_ev.get(tt - 2), first_guard], inc=sPE)
        att = pool[tt % 2]
        xb = pool[2 + tt % 2]
        x2 = pool[4 + tt % 2]
        tok_att_ev[tt] = r.add("vector", (lambda att, ps: lambda e: e.tensor_copy(att[:], ps[:, :]))(att, ps),
                               waits=[tok_att], inc=sVE)
        tok_x = r.add("sync", (lambda xb, tt: lambda e: e.dma_start(
            out=xb[:], in_=x_in[tt * 128:(tt + 1) * 128, :]))(xb, tt),
            waits=[tok_x2.get(tt - 2),
                   tok_cn.get(14) if tt == 0 else None,
                   tok_cn.get(15) if tt == 1 else None], inc=s_x[tt % 2], amt=16)
        r.add("vector", (lambda att: lambda e: e.reduce_sum(cols[:, 0:1], att[:], axis=AX.X))(att))
        r.add("vector", (lambda att, x2: lambda e: e.tensor_mul(x2[:], att[:], att[:]))(att, x2),
              waits=[tok_x2w.get(tt - 2)])
        r.add("vector", (lambda x2: lambda e: e.reduce_sum(cols[:, 1:2], x2[:], axis=AX.X))(x2))
        r.add("vector", lambda e: e.tensor_scalar(cols[:, 2:3], cols[:, 0:1], cst[:, C_INV2048:C_INV2048 + 1], None, ALU.mult))
        r.add("vector", lambda e: e.drain())
        r.add("vector", lambda e: e.tensor_mul(cols[:, 3:4], cols[:, 2:3], cols[:, 2:3]))
        r.add("vector", lambda e: e.drain())
        tok_v = r.add("vector", lambda e: e.tensor_scalar(cols[:, 4:5], cols[:, 1:2], cst[:, C_INV2048:C_INV2048 + 1],
                                                          cols[:, 3:4], ALU.mult, ALU.subtract), inc=sVE)
        r.add("scalar", lambda e: e.activation(cols[:, 5:6], cols[:, 4:5], AF.Ln, bias=cst[:, C_EPS_LN:C_EPS_LN + 1]),
              waits=[tok_v])
        r.add("scalar", lambda e: e.drain())
        tok_rstd = r.add("scalar", lambda e: e.activation(cols[:, 5:6], cols[:, 5:6], AF.Exp,
                                                          scale=cst[:, C_NEG_HALF:C_NEG_HALF + 1]), inc=sSE)
        r.add("vector", (lambda att: lambda e: e.tensor_scalar(att[:], att[:], cols[:, 2:3], cols[:, 5:6],
                                                               ALU.subtract, ALU.mult))(att), waits=[tok_rstd])
        tok_x2[tt] = r.add("vector", (lambda x2, att, xb: lambda e: e.tensor_add(x2[:], att[:], xb[:]))(x2, att, xb),
                           waits=[tok_x], inc=sVE)
        r.add("vector", (lambda x2: lambda e: e.reduce_sum(cols[:, 8:9], x2[:], axis=AX.X))(x2))
        r.add("vector", (lambda att, x2: lambda e: e.tensor_mul(att[:], x2[:], x2[:]))(att, x2))
        r.add("vector", (lambda att: lambda e: e.reduce_sum(cols[:, 9:10], att[:], axis=AX.X))(att))
        r.add("vector", lambda e: e.tensor_scalar(cols[:, 10:11], cols[:, 8:9], cst[:, C_INV2048:C_INV2048 + 1], None, ALU.mult))
        r.add("vector", lambda e: e.drain())
        r.add("vector", lambda e: e.tensor_mul(cols[:, 11:12], cols[:, 10:11], cols[:, 10:11]))
        r.add("vector", lambda e: e.drain())
        tok_v2 = r.add("vector", lambda e: e.tensor_scalar(cols[:, 12:13], cols[:, 9:10], cst[:, C_INV2048:C_INV2048 + 1],
                                                           cols[:, 11:12], ALU.mult, ALU.subtract), inc=sVE)
        r.add("scalar", lambda e: e.activation(cols[:, 13:14], cols[:, 12:13], AF.Ln, bias=cst[:, C_EPS_LN:C_EPS_LN + 1]),
              waits=[tok_v2])
        r.add("scalar", lambda e: e.drain())
        tok_rstd2 = r.add("scalar", lambda e: e.activation(cols[:, 13:14], cols[:, 13:14], AF.Exp,
                                                           scale=cst[:, C_NEG_HALF:C_NEG_HALF + 1]), inc=sSE)
        tok_x2ln = r.add("vector", (lambda tt, x2: lambda e: e.tensor_scalar(
            obuf[tt % 2][:], x2[:], cols[:, 10:11], cols[:, 13:14], ALU.subtract, ALU.mult))(tt, x2),
            waits=[tok_rstd2, tok_x2lnw.get(tt - 1), tok_h.get(15) if tt == 0 else None], inc=sVE)
        tok_x2w[tt] = r.add("scalar", (lambda x2, tt: lambda e: e.dma_start(
            out=x2_hbm[tt * 128:(tt + 1) * 128, :], in_=x2[:]))(x2, tt), waits=[tok_x2[tt]], inc=s_x2w[tt % 2], amt=16)
        last = None
        for dt in range(NKD):
            last = r.add("scalar", (lambda dt, tt: lambda e: e.dma_start(
                out=x2ln_hbm[dt, tt * 128:(tt + 1) * 128, :], in_=obuf[tt % 2][:, dt * 128:(dt + 1) * 128]))(dt, tt),
                waits=[tok_x2ln], inc=s_x2lnw, amt=16)
        tok_x2lnw[tt] = last

    tok_tr2 = None
    for dt in range(NKD):
        tok_tr2 = r.add("sync", (lambda dt: lambda e: e.dma_start(
            out=A[:, dt * R:(dt + 1) * R], in_=x2ln_hbm[dt, :, :], transpose=True))(dt),
            waits=[tok_x2lnw[NT - 1], tok_att if dt == 0 else None], inc=s_tr2, amt=16)

    wslot = [BIG.ap()[:, i * D:(i + 1) * D] for i in range(4)]
    gslot = [BIG.ap()[:, (4 + i) * D:(5 + i) * D] for i in range(2)]
    tok_pair = {}
    tok_gst = {}
    last_pair = None
    last_silu = None
    for ft in range(NF):
        wg_v = wslot[(ft % 2) * 2 + 0]
        wu_v = wslot[(ft % 2) * 2 + 1]
        tok_wg = r.add("gpsimd", (lambda ft, wg_v: lambda e: e.dma_start(
            out=wg_v.rearrange("p (kt m) -> p kt m", m=128), in_=WFG[ft, :, :, :]))(ft, wg_v),
            waits=[tok_pair.get((ft - 2, 3)), tok_att if ft == 0 else None], inc=s_wg[ft % 2], amt=16)
        tok_wu = r.add("gpsimd", (lambda ft, wu_v: lambda e: e.dma_start(
            out=wu_v.rearrange("p (kt m) -> p kt m", m=128), in_=WFU[ft, :, :, :]))(ft, wu_v),
            inc=s_wu[ft % 2], amt=16)
        gs = gslot[ft % 2]
        for nch in range(NCH):
            slot = (nch % 2) * 2
            psg = PS_A[:, slot * 512:(slot + 1) * 512]
            psu = PS_A[:, (slot + 1) * 512:(slot + 2) * 512]

            def gmm(e, wv=wg_v, psx=psg, nch=nch):
                inst = None
                for kt in range(NKD):
                    inst = e.matmul(psx, wv[:, kt * 128:(kt + 1) * 128],
                                    A[:, kt * R + nch * 512: kt * R + nch * 512 + 512],
                                    start=(kt == 0), stop=(kt == NKD - 1))
                return inst

            def umm(e, wv=wu_v, psx=psu, nch=nch):
                inst = None
                for kt in range(NKD):
                    inst = e.matmul(psx, wv[:, kt * 128:(kt + 1) * 128],
                                    A[:, kt * R + nch * 512: kt * R + nch * 512 + 512],
                                    start=(kt == 0), stop=(kt == NKD - 1))
                return inst
            prev = tok_pair.get((ft - 1, nch + 2) if nch < 2 else (ft, nch - 2))
            guard0 = None
            guard1 = tok_att_ev.get(NT - 2) if (ft == 0 and nch == 0) else None
            tok_g = r.add("tensor", gmm, waits=[tok_wg, tok_tr2, prev, guard0, guard1], inc=sPE)
            tok_um = r.add("tensor", umm, waits=[tok_wu], inc=sPE)
            tok_sg = r.add("scalar", (lambda psg: lambda e: e.activation(pool[0][:, 0:512], psg, AF.Silu))(psg),
                           waits=[tok_g, last_pair], inc=sSE)
            last_silu = tok_sg
            tok_pair[(ft, nch)] = r.add("vector", (lambda gs, nch, psu: lambda e: e.tensor_tensor(
                gs[:, nch * 512:(nch + 1) * 512], pool[0][:, 0:512], psu, ALU.mult))(gs, nch, psu),
                waits=[tok_sg, tok_um, tok_gst.get(ft - 2)], inc=sVE)
            last_pair = tok_pair[(ft, nch)]
        last = None
        for tt in range(NT):
            last = r.add("scalar", (lambda tt, ft, gs: lambda e: e.dma_start(
                out=gated_hbm[tt, :, ft, :], in_=gs[:, tt * 128:(tt + 1) * 128]))(tt, ft, gs),
                waits=[tok_pair[(ft, 3)]], inc=s_gst[ft % 2], amt=16)
        tok_gst[ft] = last

    gt_sl = [A.ap()[:, 0:NF * 128], A.ap()[:, NF * 128:2 * NF * 128]]
    tok_dn_ev = {}
    tok_ffw = {}
    last_gt_user = {}
    for dq in range(4):
        tok_wd = None
        for kt in range(NF):
            tok_wd = r.add("gpsimd", (lambda dq, kt: lambda e: e.dma_start(
                out=BIG[:, kt * 512:(kt + 1) * 512], in_=WD[kt, :, dq * 512:(dq + 1) * 512]))(dq, kt),
                waits=[last_pair if (dq == 0 and kt == 0) else None,
                       tok_gst.get(NF - 1) if (dq == 0 and kt == 0) else None,
                       tok_gst.get(NF - 2) if (dq == 0 and kt == 0) else None,
                       tok_dn_ev.get((dq - 1, NT - 1)) if kt == 0 else None,
                       tok_dn_ev.get((dq - 1, NT - 2)) if kt == 0 else None], inc=s_wd, amt=16)
        for tt in range(NT):
            gv = gt_sl[tt % 2]
            tok_gtl = r.add("sync", (lambda tt, gv: lambda e: e.dma_start(
                out=gv.rearrange("p (f c) -> p f c", c=128), in_=gated_hbm[tt, :, :, :]))(tt, gv),
                waits=[tok_gst.get(NF - 1) if (dq == 0 and tt == 0) else None,
                       tok_gst.get(NF - 2) if (dq == 0 and tt == 0) else None,
                       last_pair if (dq == 0 and tt < 2) else None,
                       last_gt_user.get(tt % 2)], inc=s_gt[tt % 2], amt=16)
            ps = PS_B[:, (tt % 2) * 512:(tt % 2 + 1) * 512]

            def dnmm(e, gv=gv, psx=ps):
                inst = None
                for kt in range(NF):
                    inst = e.matmul(psx, gv[:, kt * 128:(kt + 1) * 128],
                                    BIG[:, kt * 512:(kt + 1) * 512],
                                    start=(kt == 0), stop=(kt == NF - 1))
                return inst
            tok_dn = r.add("tensor", dnmm,
                           waits=[tok_gtl, tok_wd, tok_dn_ev.get((dq, tt - 2)),
                                  tok_dn_ev.get((dq - 1, NT - 2 + (tt % 2))) if tt < 2 else None],
                           inc=sPE)
            last_gt_user[tt % 2] = tok_dn
            ev = pool[tt % 2]
            tok_ev = r.add("vector", (lambda ev, psx: lambda e: e.tensor_copy(ev[:, 0:512], psx))(ev, ps),
                           waits=[tok_dn, tok_ffw.get((dq, tt - 2)),
                                  tok_ffw.get((dq - 1, NT - 2 + (tt % 2))) if tt < 2 else None,
                                  last_silu if (dq == 0 and tt < 2) else None],
                           inc=sVE)
            tok_dn_ev[(dq, tt)] = tok_ev
            tok_ffw[(dq, tt)] = r.add("scalar", (lambda ev, tt, dq: lambda e: e.dma_start(
                out=ffnout_hbm[tt * 128:(tt + 1) * 128, dq * 512:(dq + 1) * 512], in_=ev[:, 0:512]))(ev, tt, dq),
                waits=[tok_ev], inc=s_ffw[tt % 2], amt=16)

    tok_outw = {}
    tok_ot = {}
    for tt in range(NT):
        ffn = pool[tt % 2]
        x2 = pool[2 + tt % 2]
        ot = pool[4 + tt % 2]
        ffw_done = ([tok_ffw[(dq, tp)] for dq in range(4) for tp in (NT - 2, NT - 1)]
                    if tt == 0 else [])
        tok_lf = r.add("sync", (lambda ffn, tt: lambda e: e.dma_start(
            out=ffn[:], in_=ffnout_hbm[tt * 128:(tt + 1) * 128, :]))(ffn, tt),
            waits=ffw_done + [tok_outw.get(tt - 2),
                              tok_dn_ev.get((3, NT - 2 + (tt % 2))) if tt < 2 else None],
            inc=s_lf[tt % 2], amt=16)
        tok_lx2 = r.add("sync", (lambda x2, tt: lambda e: e.dma_start(
            out=x2[:], in_=x2_hbm[tt * 128:(tt + 1) * 128, :]))(x2, tt), inc=s_lx2[tt % 2], amt=16)
        r.add("vector", (lambda ffn: lambda e: e.reduce_sum(cols[:, 0:1], ffn[:], axis=AX.X))(ffn),
              waits=[tok_lf])
        r.add("vector", (lambda ffn, ot: lambda e: e.tensor_mul(ot[:], ffn[:], ffn[:]))(ffn, ot),
              waits=[tok_outw.get(tt - 2),
                     tok_x2w.get(NT - 2 + (tt % 2)) if tt < 2 else None])
        r.add("vector", (lambda ot: lambda e: e.reduce_sum(cols[:, 1:2], ot[:], axis=AX.X))(ot))
        r.add("vector", lambda e: e.tensor_scalar(cols[:, 2:3], cols[:, 0:1], cst[:, C_INV2048:C_INV2048 + 1], None, ALU.mult))
        r.add("vector", lambda e: e.drain())
        r.add("vector", lambda e: e.tensor_mul(cols[:, 3:4], cols[:, 2:3], cols[:, 2:3]))
        r.add("vector", lambda e: e.drain())
        tok_v = r.add("vector", lambda e: e.tensor_scalar(cols[:, 4:5], cols[:, 1:2], cst[:, C_INV2048:C_INV2048 + 1],
                                                          cols[:, 3:4], ALU.mult, ALU.subtract), inc=sVE)
        r.add("scalar", lambda e: e.activation(cols[:, 5:6], cols[:, 4:5], AF.Ln, bias=cst[:, C_EPS_LN:C_EPS_LN + 1]),
              waits=[tok_v])
        r.add("scalar", lambda e: e.drain())
        tok_rstd = r.add("scalar", lambda e: e.activation(cols[:, 5:6], cols[:, 5:6], AF.Exp,
                                                          scale=cst[:, C_NEG_HALF:C_NEG_HALF + 1]), inc=sSE)
        r.add("vector", (lambda ffn: lambda e: e.tensor_scalar(ffn[:], ffn[:], cols[:, 2:3], cols[:, 5:6],
                                                               ALU.subtract, ALU.mult))(ffn), waits=[tok_rstd])
        tok_ot[tt] = r.add("vector", (lambda ot, ffn, x2: lambda e: e.tensor_add(ot[:], ffn[:], x2[:]))(ot, ffn, x2),
                           waits=[tok_lx2], inc=sVE)
        tok_outw[tt] = r.add("sync", (lambda ot, tt: lambda e: e.dma_start(
            out=out[tt * 128:(tt + 1) * 128, :], in_=ot[:]))(ot, tt), waits=[tok_ot[tt]], inc=s_outw[tt % 2], amt=16)

    r.add("sync", lambda e: e.nop(), waits=r.final_dma_waits())

    with nc.Block() as block:
        r.emit(nc, block)
    return nc



def build_fused():
    nc = bass.Bass("TRN2", target_bir_lowering=False)
    xf = nc.dram_tensor("xf", [S, D], F32, kind="ExternalInput").ap()
    x_in = nc.dram_tensor("x", [R, D], F32, kind="ExternalInput").ap()
    WI = nc.dram_tensor("WI", [NKD, 128, NKD, 128], F32, kind="ExternalInput").ap()
    WG = nc.dram_tensor("WG", [NG, 128, NKD, 128], F32, kind="ExternalInput").ap()
    bgT = nc.dram_tensor("bgT", [128, NG], F32, kind="ExternalInput").ap()
    mselt = nc.dram_tensor("mselt", [128, 8], F32, kind="ExternalInput").ap()
    rmsT = nc.dram_tensor("rmsT", [128, 4], F32, kind="ExternalInput").ap()
    WO = nc.dram_tensor("WO", [NKD, 128, D], F32, kind="ExternalInput").ap()
    WFG = nc.dram_tensor("WFG", [NF, 128, NKD, 128], F32, kind="ExternalInput").ap()
    WFU = nc.dram_tensor("WFU", [NF, 128, NKD, 128], F32, kind="ExternalInput").ap()
    WD = nc.dram_tensor("WD", [NF, 128, D], F32, kind="ExternalInput").ap()
    consts = nc.dram_tensor("consts", [128, 8], F32, kind="ExternalInput").ap()
    out = nc.dram_tensor("out", [R, D], F32, kind="ExternalOutput").ap()

    xlnf_hbm = nc.dram_tensor("xlnf_hbm", [NKD, 7 * R, 128], BF16).ap()
    xlno_hbm = nc.dram_tensor("xlno_hbm", [NKD, R, 128], BF16).ap()
    sp_o = nc.dram_tensor("sp_o", [16, 128, R], BF16).ap()
    x2_hbm = nc.dram_tensor("x2_hbm", [R, D], F32).ap()
    x2ln_hbm = nc.dram_tensor("x2ln_hbm", [NKD, R, 128], BF16).ap()
    gated_hbm = nc.dram_tensor("gated_hbm", [NT, 128, NF, 128], BF16).ap()
    ffnout_hbm = nc.dram_tensor("ffnout_hbm", [R, D], F32).ap()

    r = Rec()
    sVE = r.sem("sVE", "vector")
    sSE = r.sem("sSE", "scalar")
    sPE = r.sem("sPE", "tensor")
    s_cst = r.sem("s_cst")
    s_sx = r.sem("s_sx")
    s_ax = r.sem("s_ax")
    s_st = r.dsems("s_st", 2)
    s_tr = r.sem("s_tr")
    s_w = r.dsems("s_w", 2)
    s_spo = r.dsems("s_spo", 2)
    s_lo = r.dsems("s_lo", 2)
    s_wo = r.sem("s_wo")
    s_x = r.dsems("s_x", 2)
    s_x2w = r.dsems("s_x2w", 2)
    s_x2lnw = r.sem("s_x2lnw")
    s_tr2 = r.sem("s_tr2")
    s_wg = r.dsems("s_wg", 2)
    s_wu = r.dsems("s_wu", 2)
    s_gst = r.dsems("s_gst", 2)
    s_wd = r.dsems("s_wd", 2)
    s_gt = r.dsems("s_gt", 2)
    s_ffw = r.dsems("s_ffw", 2)
    s_lf = r.dsems("s_lf", 2)
    s_lx2 = r.dsems("s_lx2", 2)
    s_outw = r.dsems("s_outw", 2)

    st = contextlib.ExitStack()
    A = st.enter_context(nc.sbuf_tensor("A", [128, NKD * R], BF16))
    XB = st.enter_context(nc.sbuf_tensor("XB", [128, NKD * R], BF16))
    pool = [st.enter_context(nc.sbuf_tensor(f"T{i}", [128, R], F32)) for i in range(7)]
    cbuf = [st.enter_context(nc.sbuf_tensor(f"cb{i}", [128, R], BF16)) for i in range(2)]
    obuf0 = st.enter_context(nc.sbuf_tensor("ob0", [128, R], BF16))
    obuf = [obuf0, obuf0]
    wbuf = [st.enter_context(nc.sbuf_tensor(f"wb{i}", [128, NKD * 128], BF16)) for i in range(2)]
    statsb = st.enter_context(nc.sbuf_tensor("statsb", [128, 256], F32))
    state = st.enter_context(nc.sbuf_tensor("state", [128, 16], F32))
    cand = st.enter_context(nc.sbuf_tensor("cand", [128, 112], BF16))
    cin_sb = st.enter_context(nc.sbuf_tensor("cin_sb", [128, 16], F32))
    ta_sb = st.enter_context(nc.sbuf_tensor("ta_sb", [128, 16], F32))
    cols = st.enter_context(nc.sbuf_tensor("cols", [128, 16], F32))
    cst = st.enter_context(nc.sbuf_tensor("cst", [128, 8], F32))
    bg_sb = st.enter_context(nc.sbuf_tensor("bg_sb", [128, NG], F32))
    m_sb = st.enter_context(nc.sbuf_tensor("m_sb", [128, 8], F32))
    rms_sb = st.enter_context(nc.sbuf_tensor("rms_sb", [128, 4], F32))
    ones_c = st.enter_context(nc.sbuf_tensor("ones_c", [128, 1], F32))
    ones_r = st.enter_context(nc.sbuf_tensor("ones_r", [1, 128], BF16))
    rmsrow = st.enter_context(nc.sbuf_tensor("rmsrow", [1, 512], BF16))
    wrm = st.enter_context(nc.sbuf_tensor("wrm", [128, 8], F32))
    PS_A = st.enter_context(nc.psum_tensor("PS_A", [128, D], F32))
    PS_B = st.enter_context(nc.psum_tensor("PS_B", [128, D], F32))
    ps_mm = [PS_A.ap()[:, 0:512], PS_A.ap()[:, 512:1024]]
    sxb = A.ap()[:, 14 * R:15 * R]
    dump = A.ap()[:, 15 * R:16 * R]

    r.add("sync", lambda e: e.dma_start(out=cst[:], in_=consts[:, :]), inc=s_cst, amt=16)
    r.add("sync", lambda e: e.dma_start(out=bg_sb[:], in_=bgT[:, :]), inc=s_cst, amt=16)
    r.add("sync", lambda e: e.dma_start(out=m_sb[:], in_=mselt[:, :]), inc=s_cst, amt=16)
    tok_cst = r.add("sync", lambda e: e.dma_start(out=rms_sb[:], in_=rmsT[:, :]), inc=s_cst, amt=16)
    r.add("scalar", lambda e: e.copy(wrm[:, 0:1], cst[:, 0:1]), waits=[tok_cst])
    r.add("scalar", lambda e: e.copy(wrm[:, 1:2], cst[:, 0:1]))
    r.add("vector", lambda e: e.memset(wrm[:, 2:3], 0.0))
    r.add("gpsimd", lambda e: e.memset(wrm[:, 4:5], 0.0))
    r.add("vector", lambda e: e.memset(ones_c[:], 1.0))
    r.add("vector", lambda e: e.memset(ones_r[:], 1.0))
    r.add("vector", lambda e: e.memset(state[:], 0.0))
    tok_init = r.add("vector", lambda e: e.memset(cin_sb[:], 0.0), inc=sVE)

    def src_tile(gt):
        return xf[gt * 128:(gt + 1) * 128, :] if gt < 112 else x_in[(gt - 112) * 128:(gt - 111) * 128, :]

    tok_wave = {}
    tok_statdone = {}
    st_ve = [None]
    st_se = [None]
    tok_apl = [None]
    tok_store = {}

    def emit_stats_tile(sblk, tt):
        gt = sblk * 16 + tt
        tok_l = r.add("gpsimd", (lambda gt: lambda e: e.dma_start(out=sxb, in_=src_tile(gt)))(gt),
                      waits=[st_ve[0], st_se[0], tok_cst if gt == 0 else None], inc=s_sx, amt=16)
        st_se[0] = r.add("scalar", (lambda gt: lambda e: e.activation(
            obuf0[:], sxb, AF.Square, accum_out=statsb[:, 128 + gt:129 + gt]))(gt),
            waits=[tok_l], inc=sSE)
        st_ve[0] = r.add("vector", (lambda gt: lambda e: e.reduce_sum(
            statsb[:, gt:gt + 1], sxb, axis=AX.X))(gt), waits=[tok_l], inc=sVE)

    def emit_wave(sblk, q):
        c0 = sblk * 16 + 4 * q
        r.add("vector", (lambda c0: lambda e: e.tensor_scalar(
            statsb[:, c0:c0 + 4], statsb[:, c0:c0 + 4], cst[:, C_INV2048:C_INV2048 + 1], None, ALU.mult))(c0),
            waits=[st_se[0]])
        r.add("vector", (lambda c0: lambda e: e.tensor_mul(
            ta_sb[:, 0:4], statsb[:, c0:c0 + 4], statsb[:, c0:c0 + 4]))(c0))
        r.add("vector", (lambda c0: lambda e: e.tensor_scalar(
            statsb[:, 128 + c0:132 + c0], statsb[:, 128 + c0:132 + c0],
            cst[:, C_INV2048:C_INV2048 + 1], None, ALU.mult))(c0))
        tok_vv = r.add("vector", (lambda c0: lambda e: e.tensor_sub(
            statsb[:, 128 + c0:132 + c0], statsb[:, 128 + c0:132 + c0], ta_sb[:, 0:4]))(c0), inc=sVE)
        r.add("scalar", (lambda c0: lambda e: e.activation(
            statsb[:, 128 + c0:132 + c0], statsb[:, 128 + c0:132 + c0], AF.Ln,
            bias=cst[:, C_EPS_LN:C_EPS_LN + 1]))(c0), waits=[tok_vv])
        r.add("scalar", lambda e: e.drain())
        tok_wave[(sblk, q)] = r.add("scalar", (lambda c0: lambda e: e.activation(
            statsb[:, 128 + c0:132 + c0], statsb[:, 128 + c0:132 + c0], AF.Exp,
            scale=cst[:, C_NEG_HALF:C_NEG_HALF + 1]))(c0), inc=sSE)
        tok_statdone[sblk] = (st_se[0], st_ve[0])

    def emit_apply_tile(sblk, tt):
        if True:
            gt = sblk * 16 + tt
            sl = gt % 2
            tok_l = r.add("sync", (lambda gt: lambda e: e.dma_start(out=pool[6][:], in_=src_tile(gt)))(gt),
                          waits=[tok_apl[0]], inc=s_ax, amt=16)
            tok_apl[0] = r.add("vector", (lambda gt, sl: lambda e: e.tensor_scalar(
                cbuf[sl][:], pool[6][:], statsb[:, gt:gt + 1], statsb[:, 128 + gt:129 + gt],
                ALU.subtract, ALU.mult))(gt, sl),
                waits=[tok_l, tok_wave[(sblk, (gt % 16) // 4)], tok_store.get(gt - 2)], inc=sVE)
            tok_w = None
            for dt in range(NKD):
                if gt < 112:
                    dst = xlnf_hbm[dt, gt * 128:(gt + 1) * 128, :]
                else:
                    dst = xlno_hbm[dt, (gt - 112) * 128:(gt - 111) * 128, :]
                tok_w = r.add("sync", (lambda dst, sl, dt: lambda e: e.dma_start(
                    out=dst, in_=cbuf[sl][:, dt * 128:(dt + 1) * 128]))(dst, sl, dt),
                    waits=[tok_apl[0]], inc=s_st[sl], amt=16)
            tok_store[gt] = tok_w

    ev_tok = {}
    psum_i = [0]
    last_w_user = {}

    tile_i = [0]

    def emit_gemm_tile(weight_dram_block, rhs_sbuf, out_cb, extra_mm_waits=(), se_extra=(),
                       fast=True):
        wslot = tile_i[0] % 2
        wv = wbuf[wslot]
        tok_w = r.add("gpsimd", (lambda wv, weight_dram_block: lambda e: e.dma_start(
            out=wv.ap().rearrange("p (kt m) -> p kt m", m=128), in_=weight_dram_block))(wv, weight_dram_block),
            waits=[last_w_user.get(wslot)], inc=s_w[wslot], amt=16)
        last_e = None
        last_m = None
        if fast:
            ti = tile_i[0]
            tile_i[0] += 1
            psf = PS_A if ti % 2 == 0 else PS_B

            def mmall(e, wv=wv, psf=psf, rhs_sbuf=rhs_sbuf):
                inst = None
                for kt in range(NKD):
                    for nch in range(NCH):
                        inst = e.matmul(psf[:, nch * 512:(nch + 1) * 512],
                                        wv[:, kt * 128:(kt + 1) * 128],
                                        rhs_sbuf[:, kt * R + nch * 512: kt * R + nch * 512 + 512],
                                        start=(kt == 0), stop=(kt == NKD - 1))
                return inst
            tok_m = r.add("tensor", mmall,
                          waits=[tok_w, ev_tok.get(("t", ti - 2))] + list(extra_mm_waits), inc=sPE)
            last_m = tok_m
            for nch in range(NCH):
                eng, fn = out_cb(nch, psf[:, nch * 512:(nch + 1) * 512])
                tok_e = r.add(eng, fn, waits=[tok_m] + list(se_extra),
                              inc=(sSE if eng == "scalar" else sVE))
                last_e = tok_e
            ev_tok[("t", ti)] = last_e
            psum_i[0] += NCH
        else:
            for nch in range(NCH):
                i = psum_i[0]
                psum_i[0] += 1
                ps = ps_mm[i % 2]

                def mmgroup(e, wv=wv, ps=ps, nch=nch, rhs_sbuf=rhs_sbuf):
                    inst = None
                    for kt in range(NKD):
                        inst = e.matmul(ps, wv[:, kt * 128:(kt + 1) * 128],
                                        rhs_sbuf[:, kt * R + nch * 512: kt * R + nch * 512 + 512],
                                        start=(kt == 0), stop=(kt == NKD - 1))
                    return inst
                tok_m = r.add("tensor", mmgroup,
                              waits=[tok_w, ev_tok.get(i - 2),
                                     ev_tok.get(("t", tile_i[0] - 1)),
                                     ev_tok.get(("t", tile_i[0] - 2))] + list(extra_mm_waits), inc=sPE)
                last_m = tok_m
                eng, fn = out_cb(nch, ps)
                tok_e = r.add(eng, fn, waits=[tok_m] + list(se_extra), inc=(sSE if eng == "scalar" else sVE))
                ev_tok[i] = tok_e
                last_e = tok_e
            tile_i[0] += 1
        last_w_user[wslot] = last_m
        return last_e, last_m

    def gate_cb(func, out_ap_fn, gt):
        def cb(nch, ps):
            return ("scalar", (lambda nch, ps: lambda e: e.activation(
                out_ap_fn(nch), ps, func, bias=bg_sb[:, gt:gt + 1]))(nch, ps))
        return cb

    tok_scan = {}
    tok_u = {}
    tok_spill = {}
    tok_oload = {}
    prev_p2_last = [None]
    tok_cin = None
    tok_h = {}
    tok_cn = {}
    tok_rmsln_box = [None]
    prev_psb_reader_box = [None]
    tok_own_scan = {}

    def emit_q1_head(h):
        toks_sq = {}
        for j in range(4):
            ct_idx = h * 4 + j
            tok_sq = r.add("scalar", (lambda ct_idx: lambda e: e.activation(
                pool[4][:], A[:, ct_idx * R:(ct_idx + 1) * R], AF.Square))(ct_idx),
                waits=[tok_own_scan[ct_idx], toks_sq.get(j - 1)], inc=sSE)

            def ssq(e, j=j):
                inst = None
                for nch in range(NCH):
                    inst = e.matmul(PS_B[0:1, nch * 512:(nch + 1) * 512], ones_c[:, :],
                                    pool[4][:, nch * 512:(nch + 1) * 512],
                                    start=(j == 0), stop=(j == 3))
                return inst
            toks_sq[j] = r.add("tensor", ssq,
                               waits=[tok_sq,
                                      (prev_psb_reader_box[0] if j == 0 else None)], inc=sPE)
        tok_bc = None
        tok_inv = None
        for nch in range(NCH):
            r.add("scalar", (lambda nch: lambda e: e.activation(
                rmsrow[:], PS_B[0:1, nch * 512:(nch + 1) * 512], AF.Ln,
                scale=cst[0:1, C_INV512:C_INV512 + 1],
                bias=cst[0:1, C_EPS_RMS:C_EPS_RMS + 1]))(nch),
                waits=[toks_sq[3], tok_bc])
            r.add("scalar", lambda e: e.drain())
            tok_inv = r.add("scalar", lambda e: e.activation(rmsrow[:], rmsrow[:], AF.Exp,
                                                             scale=cst[0:1, C_NEG_HALF:C_NEG_HALF + 1]), inc=sSE)
            tok_bc = r.add("tensor", (lambda nch: lambda e: e.matmul(
                PS_B[:, nch * 512:(nch + 1) * 512], ones_r[:, :], rmsrow[0:1, :],
                start=True, stop=True))(nch),
                waits=[tok_inv], inc=sPE)
        tok_rmsln_box[0] = tok_inv
        for j in range(4):
            ct_idx = h * 4 + j
            tok_lo = r.add("sync", (lambda cbv, ct_idx: lambda e: e.dma_start(
                out=cbv[:], in_=sp_o[ct_idx, :, :]))(cbuf[j % 2], ct_idx),
                waits=[tok_spill[("o", ct_idx + 2 if j < 2 else ct_idx)],
                       tok_h.get(ct_idx - 2)], inc=s_lo[ct_idx % 2], amt=16)
            tok_oload[ct_idx] = tok_lo
            tok_cn[ct_idx] = r.add("vector", (lambda j, ct_idx: lambda e: e.scalar_tensor_tensor(
                pool[4][:], A[:, ct_idx * R:(ct_idx + 1) * R], rms_sb[:, j:j + 1], PS_B[:, :],
                ALU.mult, ALU.mult))(j, ct_idx),
                waits=[tok_bc], inc=sVE)
            tok_th = r.add("scalar", lambda e: e.activation(pool[5][:], pool[4][:], AF.Tanh),
                           waits=[tok_cn[ct_idx]], inc=sSE)
            tok_h[ct_idx] = r.add("vector", (lambda ct_idx, j: lambda e: e.tensor_mul(
                A[:, ct_idx * R:(ct_idx + 1) * R], pool[5][:], cbuf[j % 2][:]))(ct_idx, j),
                waits=[tok_th, tok_lo], inc=sVE)
        prev_psb_reader_box[0] = tok_cn[h * 4 + 3]
    tok_own_scan = {}
    own_gates_last = [None, None]

    def emit_p1(s_blk, dts=tuple(range(NKD))):
        own = (s_blk == 7)
        tok_tr = None
        for dt in dts:
            if own:
                src = xlno_hbm[dt, :, :]
            else:
                src = xlnf_hbm[dt, s_blk * R:(s_blk + 1) * R, :]
            tok_tr = r.add("sync", (lambda src, dt: lambda e: e.dma_start(
                out=A[:, dt * R:(dt + 1) * R], in_=src, transpose=True))(src, dt),
                waits=[tok_store.get(s_blk * 16 + 14), tok_store.get(s_blk * 16 + 15),
                       (prev_p2_last[0] if dt == 0 else None),
                       (tok_statdone[s_blk][0] if dt == 14 else None),
                       (tok_statdone[s_blk][1] if dt == 14 else None),
                       (tok_scan.get((s_blk - 1, HEADS - 1, 3)) if dt == 15 else None)],
                inc=s_tr, amt=16)
        return tok_tr

    for q in range(4):
        for tt in range(4 * q, 4 * q + 4):
            emit_stats_tile(0, tt)
        emit_wave(0, q)
    for tt in range(16):
        emit_apply_tile(0, tt)
    tok_tr_next = [emit_p1(0)]
    p0u = {}
    p0f = {}
    for k in range(8):
        p0u.setdefault((k // 4, k % 4), []).append(("s", 2 * k))
        p0f.setdefault((k // 4, k % 4), []).append(("s", 2 * k + 1))
        if (2 * k + 1) % 4 == 3:
            p0f[(k // 4, k % 4)].append(("w", (2 * k + 1) // 4))
    aslots = ([(0, "f", 2), (0, "f", 3)]
              + [(1, "u", j) for j in range(4)] + [(1, "f", j) for j in range(4)]
              + [(2, "u", j) for j in range(4)] + [(2, "f", j) for j in range(4)])
    for t, (ah, ph, aj) in enumerate(aslots[:16]):
        (p0u if ph == "u" else p0f).setdefault((ah, aj), []).append(("a", t))

    def emit_p0(table, s_blk, h, j):
        for kind, v in table.get((h, j), []):
            if kind == "s":
                emit_stats_tile(s_blk + 1, v)
            elif kind == "w":
                emit_wave(s_blk + 1, v)
            else:
                emit_apply_tile(s_blk + 1, v)
    for s_blk in range(8):
        own = (s_blk == 7)
        tok_tr = tok_tr_next[0]
        p2_last = None
        for ht in range(NKD):
            def mk_cb(ht):
                def cb(nch, ps):
                    return ("scalar", (lambda ht, nch, ps: lambda e: e.copy(
                        XB[:, ht * R + nch * 512: ht * R + nch * 512 + 512], ps))(ht, nch, ps))
                return cb
            p2_ev, p2_last = emit_gemm_tile(WI[ht, :, :, :], A.ap(), mk_cb(ht), extra_mm_waits=[tok_tr])
        prev_p2_last[0] = p2_last
        tok_xp_done = p2_ev
        for h in range(HEADS):
            for j in range(4):
                w_poolj = (tok_own_scan.get(h * 4 + j - 4) if own and h > 0
                           else tok_scan.get((s_blk, h - 1, j)))
                tok_i, _ = emit_gemm_tile(WG[h * 16 + 0 * 4 + j, :, :, :], XB.ap(),
                                          gate_cb(AF.Exp, (lambda j: lambda nch: pool[j][:, nch * 512:(nch + 1) * 512])(j),
                                                  h * 16 + 0 * 4 + j),
                                          extra_mm_waits=[tok_xp_done],
                                          se_extra=[w_poolj,
                                                    tok_scan.get((s_blk - 1, HEADS - 1, j)) if h == 0 else None,
                                                    tok_cst],
                                          fast=not own)
                tok_ct, _ = emit_gemm_tile(WG[h * 16 + 3 * 4 + j, :, :, :], XB.ap(),
                                           gate_cb(AF.Tanh, lambda nch: pool[4][:, nch * 512:(nch + 1) * 512],
                                                   h * 16 + 3 * 4 + j),
                                           se_extra=[tok_u.get((s_blk, h, j - 1)),
                                                     tok_u.get((s_blk - 1, HEADS - 1, 3)) if (h == 0 and j == 0) else None,
                                                     (tok_cn.get(h * 4 - 1) if (own and j == 0) else None)],
                                           fast=not own)
                tok_u[(s_blk, h, j)] = r.add("vector", (lambda j: lambda e: e.tensor_mul(
                    pool[j][:], pool[j][:], pool[4][:]))(j), waits=[tok_i, tok_ct], inc=sVE)
                if s_blk < 7:
                    emit_p0(p0u, s_blk, h, j)
            for j in range(4):
                ct_idx = h * 4 + j
                w_f = [tok_scan.get((s_blk, h, j - 1)) if j > 0 else
                       (tok_scan.get((s_blk, h - 1, 3)) if h > 0 else tok_scan.get((s_blk - 1, HEADS - 1, 3))),
                       (tok_h.get(h * 4 - 1) if (own and j == 0) else None)]
                tok_f, _ = emit_gemm_tile(WG[h * 16 + 1 * 4 + j, :, :, :], XB.ap(),
                                          gate_cb(AF.Sigmoid, lambda nch: pool[5][:, nch * 512:(nch + 1) * 512],
                                                  h * 16 + 1 * 4 + j),
                                          se_extra=w_f, fast=not own)
                if not own:
                    tok_c = r.add("vector", (lambda j, ct_idx: lambda e: e.tensor_tensor_scan(
                        dump[:], pool[5][:], pool[j][:], state[:, ct_idx:ct_idx + 1], ALU.mult, ALU.add))(j, ct_idx),
                        waits=[tok_f, tok_u[(s_blk, h, j)], tok_init if s_blk == 0 else None], inc=sVE)
                    r.add("vector", (lambda ct_idx: lambda e: e.tensor_copy(
                        state[:, ct_idx:ct_idx + 1], dump[:, R - 1:R]))(ct_idx))
                    r.add("vector", lambda e: e.drain())
                    tok_scan[(s_blk, h, j)] = r.add("vector", (lambda s_blk, ct_idx: lambda e: e.tensor_copy(
                        cand[:, s_blk * 16 + ct_idx:s_blk * 16 + ct_idx + 1],
                        state[:, ct_idx:ct_idx + 1]))(s_blk, ct_idx), inc=sVE)
                    if s_blk < 7:
                        emit_p0(p0f, s_blk, h, j)
                else:
                    tok_c = r.add("vector", (lambda j, ct_idx: lambda e: e.tensor_tensor_scan(
                        A[:, ct_idx * R:(ct_idx + 1) * R], pool[5][:], pool[j][:],
                        cin_sb[:, ct_idx:ct_idx + 1], ALU.mult, ALU.add))(j, ct_idx),
                        waits=[tok_f, tok_u[(s_blk, h, j)], tok_cin, p2_last if ct_idx == 0 else None], inc=sVE)
                    tok_own_scan[ct_idx] = tok_c
                    tok_scan[(s_blk, h, j)] = tok_c
                    cbv = cbuf[j % 2]
                    tok_o, _ = emit_gemm_tile(WG[h * 16 + 2 * 4 + j, :, :, :], XB.ap(),
                                              gate_cb(AF.Sigmoid,
                                                      (lambda cbv: lambda nch: cbv[:, nch * 512:(nch + 1) * 512])(cbv),
                                                      h * 16 + 2 * 4 + j),
                                              se_extra=[tok_spill.get(("o", ct_idx - 2)),
                                                        tok_h.get(ct_idx - 2),
                                                        tok_apl[0] if ct_idx < 2 else None],
                                              fast=False)
                    tok_spill[("o", ct_idx)] = r.add("scalar", (lambda cbv, ct_idx: lambda e: e.dma_start(
                        out=sp_o[ct_idx, :, :], in_=cbv[:]))(cbv, ct_idx), waits=[tok_o],
                        inc=s_spo[ct_idx % 2], amt=16)
            if own:
                emit_q1_head(h)
            if s_blk < 7 and h == 2:
                emit_p1(s_blk + 1, dts=tuple(range(15)))
        if s_blk < 7:
            tok_tr_next[0] = emit_p1(s_blk + 1, dts=(15,))
        if s_blk == 6:
            r.add("vector", lambda e: e.nop(), waits=[tok_scan[(6, HEADS - 1, 3)], tok_cst])
            for sj in range(7):
                r.add("vector", (lambda sj: lambda e: e.tensor_scalar(
                    ta_sb[:], cand[:, sj * 16:(sj + 1) * 16], m_sb[:, sj:sj + 1], None, ALU.mult))(sj))
                r.add("vector", lambda e: e.tensor_add(cin_sb[:], cin_sb[:], ta_sb[:]))
            tok_cin = r.add("vector", lambda e: e.nop(), inc=sVE)
    own_gates_last = [last_w_user.get(0), last_w_user.get(1)]

    tok_wo = None
    for kt in range(NKD):
        tok_wo = r.add("gpsimd", (lambda kt: lambda e: e.dma_start(
            out=XB[:, kt * R:kt * R + D], in_=WO[kt, :, :]))(kt),
            waits=[own_gates_last[0] if kt == 0 else None,
                   own_gates_last[1] if kt == 0 else None], inc=s_wo, amt=16)
    tok_att_ev = {}
    tok_x2w = {}
    tok_x2lnw = {}
    tok_x2 = {}
    tok_att = None
    for tt in range(NT):
        ps = PS_A if tt % 2 == 0 else PS_B

        def attmm(e, tt=tt, ps=ps):
            inst = None
            for kt in range(NKD):
                for dch in range(NCH):
                    inst = e.matmul(ps[:, dch * 512:(dch + 1) * 512],
                                    A[:, kt * R + tt * 128: kt * R + (tt + 1) * 128],
                                    XB[:, kt * R + dch * 512: kt * R + dch * 512 + 512],
                                    start=(kt == 0), stop=(kt == NKD - 1))
            return inst
        first_guard = None
        second_guard = None
        if tt == 0:
            first_guard = ev_tok.get(psum_i[0] - 1)
            second_guard = ev_tok.get(psum_i[0] - 2)
        elif tt == 1:
            first_guard = tok_cn[15]
        tok_att = r.add("tensor", attmm,
                        waits=[tok_h[15], tok_wo, tok_att_ev.get(tt - 2), first_guard, second_guard], inc=sPE)
        att = pool[tt % 2]
        xb = pool[2 + tt % 2]
        x2 = pool[4 + tt % 2]
        tok_att_ev[tt] = r.add("vector", (lambda att, ps: lambda e: e.tensor_copy(att[:], ps[:, :]))(att, ps),
                               waits=[tok_att], inc=sVE)
        tok_x = r.add("sync", (lambda xb, tt: lambda e: e.dma_start(
            out=xb[:], in_=x_in[tt * 128:(tt + 1) * 128, :]))(xb, tt),
            waits=[tok_x2.get(tt - 2),
                   tok_own_scan.get(14) if tt == 0 else None,
                   tok_own_scan.get(15) if tt == 1 else None], inc=s_x[tt % 2], amt=16)
        r.add("vector", (lambda att: lambda e: e.reduce_sum(cols[:, 0:1], att[:], axis=AX.X))(att))
        r.add("vector", (lambda att, x2: lambda e: e.tensor_mul(x2[:], att[:], att[:]))(att, x2),
              waits=[tok_x2w.get(tt - 2)])
        r.add("vector", (lambda x2: lambda e: e.reduce_sum(cols[:, 1:2], x2[:], axis=AX.X))(x2))
        r.add("vector", lambda e: e.tensor_scalar(cols[:, 2:3], cols[:, 0:1], cst[:, C_INV2048:C_INV2048 + 1], None, ALU.mult))
        r.add("vector", lambda e: e.drain())
        r.add("vector", lambda e: e.tensor_mul(cols[:, 3:4], cols[:, 2:3], cols[:, 2:3]))
        r.add("vector", lambda e: e.drain())
        tok_v = r.add("vector", lambda e: e.tensor_scalar(cols[:, 4:5], cols[:, 1:2], cst[:, C_INV2048:C_INV2048 + 1],
                                                          cols[:, 3:4], ALU.mult, ALU.subtract), inc=sVE)
        r.add("scalar", lambda e: e.activation(cols[:, 5:6], cols[:, 4:5], AF.Ln, bias=cst[:, C_EPS_LN:C_EPS_LN + 1]),
              waits=[tok_v])
        r.add("scalar", lambda e: e.drain())
        tok_rstd = r.add("scalar", lambda e: e.activation(cols[:, 5:6], cols[:, 5:6], AF.Exp,
                                                          scale=cst[:, C_NEG_HALF:C_NEG_HALF + 1]), inc=sSE)
        r.add("vector", (lambda att: lambda e: e.tensor_scalar(att[:], att[:], cols[:, 2:3], cols[:, 5:6],
                                                               ALU.subtract, ALU.mult))(att), waits=[tok_rstd])
        tok_x2[tt] = r.add("vector", (lambda x2, att, xb: lambda e: e.tensor_add(x2[:], att[:], xb[:]))(x2, att, xb),
                           waits=[tok_x], inc=sVE)
        r.add("vector", (lambda x2: lambda e: e.reduce_sum(cols[:, 8:9], x2[:], axis=AX.X))(x2))
        r.add("vector", (lambda att, x2: lambda e: e.tensor_mul(att[:], x2[:], x2[:]))(att, x2))
        r.add("vector", (lambda att: lambda e: e.reduce_sum(cols[:, 9:10], att[:], axis=AX.X))(att))
        r.add("vector", lambda e: e.tensor_scalar(cols[:, 10:11], cols[:, 8:9], cst[:, C_INV2048:C_INV2048 + 1], None, ALU.mult))
        r.add("vector", lambda e: e.drain())
        r.add("vector", lambda e: e.tensor_mul(cols[:, 11:12], cols[:, 10:11], cols[:, 10:11]))
        r.add("vector", lambda e: e.drain())
        tok_v2 = r.add("vector", lambda e: e.tensor_scalar(cols[:, 12:13], cols[:, 9:10], cst[:, C_INV2048:C_INV2048 + 1],
                                                           cols[:, 11:12], ALU.mult, ALU.subtract), inc=sVE)
        r.add("scalar", lambda e: e.activation(cols[:, 13:14], cols[:, 12:13], AF.Ln, bias=cst[:, C_EPS_LN:C_EPS_LN + 1]),
              waits=[tok_v2])
        r.add("scalar", lambda e: e.drain())
        tok_rstd2 = r.add("scalar", lambda e: e.activation(cols[:, 13:14], cols[:, 13:14], AF.Exp,
                                                           scale=cst[:, C_NEG_HALF:C_NEG_HALF + 1]), inc=sSE)
        tok_x2ln = r.add("vector", (lambda tt, x2: lambda e: e.tensor_scalar(
            obuf[tt % 2][:], x2[:], cols[:, 10:11], cols[:, 13:14], ALU.subtract, ALU.mult))(tt, x2),
            waits=[tok_rstd2, tok_x2lnw.get(tt - 1), tok_h.get(15) if tt == 0 else None], inc=sVE)
        tok_x2w[tt] = r.add("scalar", (lambda x2, tt: lambda e: e.dma_start(
            out=x2_hbm[tt * 128:(tt + 1) * 128, :], in_=x2[:]))(x2, tt), waits=[tok_x2[tt]], inc=s_x2w[tt % 2], amt=16)
        last = None
        for dt in range(NKD):
            last = r.add("scalar", (lambda dt, tt: lambda e: e.dma_start(
                out=x2ln_hbm[dt, tt * 128:(tt + 1) * 128, :], in_=obuf[tt % 2][:, dt * 128:(dt + 1) * 128]))(dt, tt),
                waits=[tok_x2ln], inc=s_x2lnw, amt=16)
        tok_x2lnw[tt] = last

    tok_tr2 = None
    for dt in range(NKD):
        tok_tr2 = r.add("sync", (lambda dt: lambda e: e.dma_start(
            out=A[:, dt * R:(dt + 1) * R], in_=x2ln_hbm[dt, :, :], transpose=True))(dt),
            waits=[tok_x2lnw[NT - 1], tok_att if dt == 0 else None], inc=s_tr2, amt=16)

    wslot = [XB.ap()[:, i * D:(i + 1) * D] for i in range(4)]
    gslot = [XB.ap()[:, (4 + i) * D:(5 + i) * D] for i in range(2)]
    tok_pair = {}
    tok_gst = {}
    last_pair = None
    last_silu = None
    for ft in range(NF):
        wg_v = wslot[(ft % 2) * 2 + 0]
        wu_v = wslot[(ft % 2) * 2 + 1]
        tok_wg = r.add("gpsimd", (lambda ft, wg_v: lambda e: e.dma_start(
            out=wg_v.rearrange("p (kt m) -> p kt m", m=128), in_=WFG[ft, :, :, :]))(ft, wg_v),
            waits=[tok_pair.get((ft - 2, 3)), tok_att if ft == 0 else None], inc=s_wg[ft % 2], amt=16)
        tok_wu = r.add("gpsimd", (lambda ft, wu_v: lambda e: e.dma_start(
            out=wu_v.rearrange("p (kt m) -> p kt m", m=128), in_=WFU[ft, :, :, :]))(ft, wu_v),
            inc=s_wu[ft % 2], amt=16)
        gs = gslot[ft % 2]
        for nch in range(NCH):
            slot = (nch % 2) * 2
            psg = PS_A[:, slot * 512:(slot + 1) * 512]
            psu = PS_A[:, (slot + 1) * 512:(slot + 2) * 512]

            def gmm(e, wv=wg_v, psx=psg, nch=nch):
                inst = None
                for kt in range(NKD):
                    inst = e.matmul(psx, wv[:, kt * 128:(kt + 1) * 128],
                                    A[:, kt * R + nch * 512: kt * R + nch * 512 + 512],
                                    start=(kt == 0), stop=(kt == NKD - 1))
                return inst

            def umm(e, wv=wu_v, psx=psu, nch=nch):
                inst = None
                for kt in range(NKD):
                    inst = e.matmul(psx, wv[:, kt * 128:(kt + 1) * 128],
                                    A[:, kt * R + nch * 512: kt * R + nch * 512 + 512],
                                    start=(kt == 0), stop=(kt == NKD - 1))
                return inst
            prev = tok_pair.get((ft - 1, nch + 2) if nch < 2 else (ft, nch - 2))
            guard0 = None
            guard1 = tok_att_ev.get(NT - 2) if (ft == 0 and nch == 0) else None
            tok_g = r.add("tensor", gmm, waits=[tok_wg, tok_tr2, prev, guard0, guard1], inc=sPE)
            tok_um = r.add("tensor", umm, waits=[tok_wu], inc=sPE)
            tok_sg = r.add("scalar", (lambda psg: lambda e: e.activation(pool[0][:, 0:512], psg, AF.Silu))(psg),
                           waits=[tok_g, last_pair], inc=sSE)
            last_silu = tok_sg
            tok_pair[(ft, nch)] = r.add("vector", (lambda gs, nch, psu: lambda e: e.tensor_tensor(
                gs[:, nch * 512:(nch + 1) * 512], pool[0][:, 0:512], psu, ALU.mult))(gs, nch, psu),
                waits=[tok_sg, tok_um, tok_gst.get(ft - 2)], inc=sVE)
            last_pair = tok_pair[(ft, nch)]
        last = None
        for tt in range(NT):
            last = r.add("scalar", (lambda tt, ft, gs: lambda e: e.dma_start(
                out=gated_hbm[tt, :, ft, :], in_=gs[:, tt * 128:(tt + 1) * 128]))(tt, ft, gs),
                waits=[tok_pair[(ft, 3)]], inc=s_gst[ft % 2], amt=16)
        tok_gst[ft] = last

    gt_sl = [A.ap()[:, 0:NF * 128], A.ap()[:, NF * 128:2 * NF * 128]]
    wd_sl = [XB.ap(), XB.ap()]
    tok_dn_ev = {}
    tok_ffw = {}
    tok_wd_b = {}
    last_gt_user = {}
    for dq in range(4):
        tok_wd = None
        wdv = wd_sl[dq % 2]
        for kt in range(NF):
            tok_wd = r.add("gpsimd", (lambda dq, kt, wdv: lambda e: e.dma_start(
                out=wdv[:, kt * 512:(kt + 1) * 512], in_=WD[kt, :, dq * 512:(dq + 1) * 512]))(dq, kt, wdv),
                waits=[last_pair if (dq == 0 and kt == 0) else None,
                       tok_gst.get(NF - 1) if (dq == 0 and kt == 0) else None,
                       tok_gst.get(NF - 2) if (dq == 0 and kt == 0) else None,
                       tok_dn_ev.get((dq - 1, NT - 1)) if kt == 0 else None,
                       tok_dn_ev.get((dq - 1, NT - 2)) if kt == 0 else None], inc=s_wd[dq % 2], amt=16)
        tok_wd_b[dq] = tok_wd
        for tt in range(NT):
            gv = gt_sl[tt % 2]
            tok_gtl = r.add("sync", (lambda tt, gv: lambda e: e.dma_start(
                out=gv.rearrange("p (f c) -> p f c", c=128), in_=gated_hbm[tt, :, :, :]))(tt, gv),
                waits=[tok_gst.get(NF - 1) if (dq == 0 and tt == 0) else None,
                       tok_gst.get(NF - 2) if (dq == 0 and tt == 0) else None,
                       last_pair if (dq == 0 and tt < 2) else None,
                       last_gt_user.get(tt % 2)], inc=s_gt[tt % 2], amt=16)
            ps = PS_B[:, (tt % 2) * 512:(tt % 2 + 1) * 512]

            def dnmm(e, gv=gv, psx=ps, wdv=wdv):
                inst = None
                for kt in range(NF):
                    inst = e.matmul(psx, gv[:, kt * 128:(kt + 1) * 128],
                                    wdv[:, kt * 512:(kt + 1) * 512],
                                    start=(kt == 0), stop=(kt == NF - 1))
                return inst
            tok_dn = r.add("tensor", dnmm,
                           waits=[tok_gtl, tok_wd, tok_dn_ev.get((dq, tt - 2)),
                                  tok_dn_ev.get((dq - 1, NT - 2 + (tt % 2))) if tt < 2 else None],
                           inc=sPE)
            last_gt_user[tt % 2] = tok_dn
            ev = pool[tt % 2]
            tok_ev = r.add("vector", (lambda ev, psx: lambda e: e.tensor_copy(ev[:, 0:512], psx))(ev, ps),
                           waits=[tok_dn, tok_ffw.get((dq, tt - 2)),
                                  tok_ffw.get((dq - 1, NT - 2 + (tt % 2))) if tt < 2 else None,
                                  last_silu if (dq == 0 and tt < 2) else None],
                           inc=sVE)
            tok_dn_ev[(dq, tt)] = tok_ev
            tok_ffw[(dq, tt)] = r.add("scalar", (lambda ev, tt, dq: lambda e: e.dma_start(
                out=ffnout_hbm[tt * 128:(tt + 1) * 128, dq * 512:(dq + 1) * 512], in_=ev[:, 0:512]))(ev, tt, dq),
                waits=[tok_ev], inc=s_ffw[tt % 2], amt=16)

    tok_outw = {}
    tok_ot = {}
    for tt in range(NT):
        ffn = pool[tt % 2]
        x2 = pool[2 + tt % 2]
        ot = pool[4 + tt % 2]
        ffw_done = ([tok_ffw[(dq, tp)] for dq in range(4) for tp in (NT - 2, NT - 1)]
                    if tt == 0 else [])
        tok_lf = r.add("sync", (lambda ffn, tt: lambda e: e.dma_start(
            out=ffn[:], in_=ffnout_hbm[tt * 128:(tt + 1) * 128, :]))(ffn, tt),
            waits=ffw_done + [tok_outw.get(tt - 2),
                              tok_dn_ev.get((3, NT - 2 + (tt % 2))) if tt < 2 else None],
            inc=s_lf[tt % 2], amt=16)
        tok_lx2 = r.add("sync", (lambda x2, tt: lambda e: e.dma_start(
            out=x2[:], in_=x2_hbm[tt * 128:(tt + 1) * 128, :]))(x2, tt), inc=s_lx2[tt % 2], amt=16)
        r.add("vector", (lambda ffn: lambda e: e.reduce_sum(cols[:, 0:1], ffn[:], axis=AX.X))(ffn),
              waits=[tok_lf])
        tok_q6sq = r.add("scalar", (lambda ffn, tt: lambda e: e.activation(
            A[:, 2 * NF * 128 + (tt % 2) * R:2 * NF * 128 + (tt % 2 + 1) * R],
            ffn[:], AF.Square, accum_out=cols[:, 1:2]))(ffn, tt),
            waits=[tok_lf, tok_outw.get(tt - 2)], inc=sSE)
        r.add("vector", lambda e: e.nop(), waits=[tok_q6sq])
        r.add("vector", lambda e: e.tensor_scalar(cols[:, 2:3], cols[:, 0:1], cst[:, C_INV2048:C_INV2048 + 1], None, ALU.mult))
        r.add("vector", lambda e: e.drain())
        r.add("vector", lambda e: e.tensor_mul(cols[:, 3:4], cols[:, 2:3], cols[:, 2:3]))
        r.add("vector", lambda e: e.drain())
        tok_v = r.add("vector", lambda e: e.tensor_scalar(cols[:, 4:5], cols[:, 1:2], cst[:, C_INV2048:C_INV2048 + 1],
                                                          cols[:, 3:4], ALU.mult, ALU.subtract), inc=sVE)
        r.add("scalar", lambda e: e.activation(cols[:, 5:6], cols[:, 4:5], AF.Ln, bias=cst[:, C_EPS_LN:C_EPS_LN + 1]),
              waits=[tok_v])
        r.add("scalar", lambda e: e.drain())
        tok_rstd = r.add("scalar", lambda e: e.activation(cols[:, 5:6], cols[:, 5:6], AF.Exp,
                                                          scale=cst[:, C_NEG_HALF:C_NEG_HALF + 1]), inc=sSE)
        r.add("vector", (lambda ffn: lambda e: e.tensor_scalar(ffn[:], ffn[:], cols[:, 2:3], cols[:, 5:6],
                                                               ALU.subtract, ALU.mult))(ffn), waits=[tok_rstd])
        tok_ot[tt] = r.add("vector", (lambda ot, ffn, x2: lambda e: e.tensor_add(ot[:], ffn[:], x2[:]))(ot, ffn, x2),
                           waits=[tok_lx2], inc=sVE)
        tok_outw[tt] = r.add("sync", (lambda ot, tt: lambda e: e.dma_start(
            out=out[tt * 128:(tt + 1) * 128, :], in_=ot[:]))(ot, tt), waits=[tok_ot[tt]], inc=s_outw[tt % 2], amt=16)

    r.add("sync", lambda e: e.nop(), waits=r.final_dma_waits())

    with nc.Block() as block:
        r.emit(nc, block)
    return nc



def kernel(L0_attn_x, L0_attn_W_in, L0_attn_W_gate, L0_attn_b_gate, L0_attn_rms_w,
           L0_attn_W_out, L0_ffn_w_gate, L0_ffn_w_up, L0_ffn_w_down):
    x = np.ascontiguousarray(np.asarray(L0_attn_x, np.float32).reshape(S, D))
    WI_t = tile_lhsT(np.asarray(L0_attn_W_in, np.float32))
    WG_t = tile_lhsT(np.asarray(L0_attn_W_gate, np.float32))
    bgT = np.ascontiguousarray(np.asarray(L0_attn_b_gate, np.float32).reshape(NG, 128).T)
    rmsT = np.ascontiguousarray(np.asarray(L0_attn_rms_w, np.float32).reshape(4, 128).T)
    WO_p = np.ascontiguousarray(np.asarray(L0_attn_W_out, np.float32).T.reshape(NKD, 128, D))
    WFG_t = tile_lhsT(np.asarray(L0_ffn_w_gate, np.float32))
    WFU_t = tile_lhsT(np.asarray(L0_ffn_w_up, np.float32))
    WD_p = np.ascontiguousarray(np.asarray(L0_ffn_w_down, np.float32).T.reshape(NF, 128, D))
    consts = np.ascontiguousarray(np.tile(np.array(
        [1e-6, 1e-5, -0.5, 1.0 / 512, 1.0 / 2048, 0.0, 1.0, 0.0], np.float32), (128, 1)))

    nc = build_fused()
    in_maps = []
    for k in range(NC):
        msel = np.zeros((128, 8), np.float32)
        if k > 0:
            msel[:, k - 1] = 1.0
        in_maps.append({
            "xf": x, "x": x[k * R:(k + 1) * R],
            "WI": WI_t, "WG": WG_t, "bgT": bgT, "mselt": msel,
            "rmsT": rmsT, "WO": WO_p, "WFG": WFG_t, "WFU": WFU_t, "WD": WD_p,
            "consts": consts,
        })
    res = run_bass_kernel_spmd(nc, in_maps, core_ids=list(range(NC)))
    out = np.concatenate([np.asarray(res.results[k]["out"]) for k in range(NC)], axis=0)
    return out.reshape(1, S, D)
